# version 01
# baseline (speedup 1.0000x reference)
import numpy as np
from contextlib import ExitStack

import concourse.bass as bass
import concourse.mybir as mybir
from concourse.bass_utils import run_bass_kernel_spmd

F32 = mybir.dt.float32
ALU = mybir.AluOpType
ACT = mybir.ActivationFunctionType
AXX = mybir.AxisListType.X

D = 1024
NCH = 8
H = 16
HD = 64
TQ = 2048
TCTX = 3072
TKB = TCTX + TQ
TT = 512
NT_OWN = TQ // TT
NT_CTX = TCTX // TT
NT_ALL = TKB // TT
OFFS = [0, 1, 2, 3, 4, 6, 8, 12, 16, 24, 32, 48, 64, 96, 128, 192, 256, 384,
        512, 768, 1024, 1536, 2048, 3072]
NOFF = 24
NEAR_MAX = 192
LN_EPS = 1e-5
E_EXP = 6
HEXP = 2048
NHC = HEXP // 128
A4 = 4096

EXPM_A = [0.9999999999999889, 0.9999999999983087, 0.49999999999954886,
          0.1666666666460249, 0.04166666666357168, 0.008333333260441621,
          0.0013888888807935605, 0.00019841258487860976,
          2.4801576895938554e-05, 2.7556410718316194e-06,
          2.7556592217755425e-07, 2.5013109003262733e-08,
          2.0848571410855793e-09, 1.5202830621755456e-10,
          1.0899443294993002e-11]

ENGINES = ("sync", "tensor", "vector", "scalar", "gpsimd")

DEBUG = False
PHASE_LIMIT = 99


class Prog:
    def __init__(self, nc, stack):
        self.nc = nc
        self.steps = {e: [] for e in ENGINES}
        self.sem = {}
        self.cnt = {}
        self.waited = {e: {} for e in ENGINES}
        for name in ("ld", "gd", "mm", "vv", "aa", "gg"):
            self.sem[name] = stack.enter_context(nc.semaphore(name))
            self.cnt[name] = 0

    def op(self, eng, fn, waits=(), inc=None, inc_by=1):
        real = []
        for t in waits:
            if t is None:
                continue
            s, v = t
            if v > 0 and self.waited[eng].get(s, 0) < v:
                self.waited[eng][s] = v
                real.append((s, v))
        tok = None
        if inc is not None:
            self.cnt[inc] += inc_by
            tok = (inc, self.cnt[inc])
        sems = self.sem

        def closure(e, fn=fn, real=tuple(real), inc=inc, inc_by=inc_by):
            for s, v in real:
                e.wait_ge(sems[s], v)
            ins = fn(e)
            if inc is not None:
                ins.then_inc(sems[inc], inc_by)
        self.steps[eng].append(closure)
        return tok

    def wait(self, eng, waits):
        real = []
        for t in waits:
            if t is None:
                continue
            s, v = t
            if v > 0 and self.waited[eng].get(s, 0) < v:
                self.waited[eng][s] = v
                real.append((s, v))
        if not real:
            return
        sems = self.sem

        def closure(e, real=tuple(real)):
            for s, v in real:
                e.wait_ge(sems[s], v)
        self.steps[eng].append(closure)

    def barrier(self):
        toks = [(s, self.cnt[s]) for s in self.cnt]
        for e in ENGINES:
            self.wait(e, toks)

    def dma(self, eng, out, in_, waits=()):
        sem = "ld" if eng == "sync" else "gd"
        return self.op(eng, lambda e, o=out, i=in_: e.dma_start(out=o, in_=i),
                       waits=waits, inc=sem, inc_by=16)


def build_program():
    nc = bass.Bass("TRN2", target_bir_lowering=False)
    stack = ExitStack()

    def dram_in(name, shape):
        return nc.dram_tensor(name, shape, F32, kind="ExternalInput").ap()

    def dram_scr(name, shape, dbg=False):
        kind = "ExternalOutput" if (DEBUG and dbg) else "Internal"
        return nc.dram_tensor(name, shape, F32, kind=kind).ap()

    x_own = dram_in("x_own", [128, NCH, TQ])
    x_ctx = dram_in("x_ctx", [128, NCH, TCTX])
    maskc = dram_in("maskc", [128, TCTX])
    wqkv = dram_in("wqkv", [128, NCH, 3 * D])
    bqkv = dram_in("bqkv", [128, 24])
    wout = dram_in("wout", [128, NCH, D])
    bout = dram_in("bout", [128, NCH])
    wgate = dram_in("wgate", [128, NCH, D])
    bgate = dram_in("bgate", [128, NCH])
    pb_in = dram_in("pb", [128, H * NOFF])
    lng = dram_in("lng", [128, NCH])
    lnb = dram_in("lnb", [128, NCH])
    aw1 = dram_in("aw1", [128, NCH, A4])
    ab1 = dram_in("ab1", [128, 32])
    aw2 = dram_in("aw2", [128, 32, D])
    ab2 = dram_in("ab2", [128, NCH])
    gatew = dram_in("gatew", [128, NCH, E_EXP])
    gateb = dram_in("gateb", [128, E_EXP])
    ew1 = dram_in("ew1", [E_EXP, 128, NCH, HEXP])
    eb1 = dram_in("eb1", [128, E_EXP * NHC])
    ew2 = dram_in("ew2", [E_EXP, 128, NHC, D])
    eb2 = dram_in("eb2", [E_EXP, D])
    gens = dram_in("gens", [4, 128, NCH, D])
    gcw = dram_in("gcw", [128, 2 * NCH])
    gcb = dram_in("gcb", [1, 1])

    out_t = nc.dram_tensor("out", [128, NCH, TQ], F32, kind="ExternalOutput").ap()

    xn_sp = dram_scr("xn_sp", [128, NCH, TKB], dbg=True)
    q_sp = dram_scr("q_sp", [128, NCH, TQ], dbg=True)
    k_sp = dram_scr("k_sp", [128, NCH, TKB], dbg=True)
    v_sp = dram_scr("v_sp", [128, NCH, TKB], dbg=True)
    gate_sp = dram_scr("gate_sp", [128, NCH, TQ], dbg=True)
    attn_sp = dram_scr("attn_sp", [128, NCH, TQ], dbg=True)
    xn2_sp = dram_scr("xn2_sp", [128, NCH, TQ], dbg=True)
    abel_sp = dram_scr("abel_sp", [128, NCH, TQ], dbg=True)
    macc_sp = dram_scr("macc_sp", [128, NCH, TQ], dbg=True)
    expm_sp = dram_scr("expm_sp", [4, 128, NCH, D], dbg=True)
    nona_sp = dram_scr("nona_sp", [128, NCH, TQ], dbg=True)
    wtok_dbg = dram_scr("wtok_dbg", [128, 16 * E_EXP], dbg=True)
    probs_dbg = dram_scr("probs_dbg", [128, NT_OWN * 4, NOFF, H], dbg=True)

    sbuf = lambda name, shape: stack.enter_context(nc.sbuf_tensor(name, shape, F32))

    AR_W = sbuf("AR_W", [128, 32768]).ap()
    AR_X = sbuf("AR_X", [128, 12288]).ap()
    S = {}
    for name, shape in [
        ("ones_col", [128, 1]), ("ones_row", [1, 128]), ("iden", [128, 128]),
        ("iota_c", [128, 1]), ("iota_r16", [128, 16]), ("hrow", [128, 2]),
        ("sel", [128, NCH * 16]), ("bc", [16, NCH * 128]), ("jdiv", [16, 128]),
        ("bqkv", [128, 24]), ("bq8", [128, 8]), ("bout", [128, NCH]),
        ("bgate", [128, NCH]), ("pb", [128, H * NOFF]), ("lng", [128, NCH]),
        ("lnb", [128, NCH]), ("ab1", [128, 32]), ("ab2", [128, NCH]),
        ("gateb", [128, E_EXP]), ("eb1", [128, E_EXP * NHC]),
        ("eb2", [E_EXP, D]), ("gcw", [128, 2 * NCH]), ("gcb", [1, 1]),
        ("rows", [1, 5 * TT]), ("st16", [128, 64]),
        ("wtok", [128, 16 * E_EXP]), ("wfm", [E_EXP, TQ]),
        ("mrow", [128, 4 * H]), ("tmp6", [128, 2 * E_EXP]),
    ]:
        S[name] = sbuf("s_" + name, shape).ap()

    ps = [stack.enter_context(nc.psum_tensor(f"psb{i}", [128, 512], F32)) for i in range(8)]

    P = Prog(nc, stack)
    bank_free = [None] * 8

    def acquire(b):
        t = bank_free[b]
        return ps[b], ([t] if t else [])

    def release(b, tok):
        bank_free[b] = tok

    t_io = P.op("gpsimd", lambda e: e.iota(S["iota_c"], pattern=[[0, 1]],
                channel_multiplier=1, allow_small_or_imprecise_dtypes=True),
                inc="gg")
    t_io2 = P.op("gpsimd", lambda e: e.iota(S["iota_r16"], pattern=[[1, 16]],
                 channel_multiplier=0, allow_small_or_imprecise_dtypes=True),
                 inc="gg")
    t_io3 = P.op("gpsimd", lambda e: e.iota(
        S["jdiv"].rearrange("p (a b) -> p a b", a=2), pattern=[[1, 2], [0, 64]],
        channel_multiplier=0, allow_small_or_imprecise_dtypes=True), inc="gg")
    t_ir = P.op("gpsimd", lambda e: e.iota(S["iden"], pattern=[[1, 128]],
                channel_multiplier=0, allow_small_or_imprecise_dtypes=True),
                inc="gg")
    t_idn = P.op("vector", lambda e: e.tensor_scalar(
        S["iden"], S["iden"], S["iota_c"][:, 0:1], None, ALU.is_equal),
        waits=[t_ir, t_io], inc="vv")
    t_on1 = P.op("vector", lambda e: e.memset(S["ones_col"], 1.0), inc="vv")
    t_on2 = P.op("vector", lambda e: e.memset(S["ones_row"], 1.0), inc="vv")
    t_hr = P.op("vector", lambda e: e.tensor_scalar(
        S["hrow"][:, 0:1], S["iota_c"], 64.0, None, ALU.is_ge),
        waits=[t_io], inc="vv")
    sel_last = None
    for c in range(NCH):
        t1 = P.op("vector", lambda e, c=c: e.tensor_scalar(
            S["hrow"][:, 1:2], S["hrow"][:, 0:1], float(2 * c), None, ALU.add),
            waits=[t_hr, sel_last], inc="vv")
        P.wait("vector", [t1])
        sel_last = P.op("vector", lambda e, c=c: e.tensor_scalar(
            S["sel"][:, c * 16:(c + 1) * 16], S["iota_r16"],
            S["hrow"][:, 1:2], None, ALU.is_equal),
            waits=[t_io2], inc="vv")
    bc_last = None
    for c in range(NCH):
        t1 = P.op("vector", lambda e, c=c: e.tensor_scalar(
            S["bc"][:, c * 128:(c + 1) * 128], S["jdiv"], float(2 * c), None,
            ALU.add), waits=[t_io3], inc="vv")
        P.wait("vector", [t1])
        bc_last = P.op("vector", lambda e, c=c: e.tensor_scalar(
            S["bc"][:, c * 128:(c + 1) * 128], S["bc"][:, c * 128:(c + 1) * 128],
            S["iota_c"][0:16, 0:1], None, ALU.is_equal),
            waits=[t_io], inc="vv")
    for name, src in [("bqkv", bqkv), ("bout", bout), ("bgate", bgate),
                      ("pb", pb_in), ("lng", lng), ("lnb", lnb), ("ab1", ab1),
                      ("ab2", ab2), ("gateb", gateb), ("eb1", eb1),
                      ("eb2", eb2), ("gcw", gcw), ("gcb", gcb)]:
        P.dma("sync", S[name], src)
    ld_all = ("ld", P.cnt["ld"])
    t_bq8 = P.op("vector", lambda e: e.tensor_scalar(
        S["bq8"], S["bqkv"][:, 0:8], 0.125, None, ALU.mult),
        waits=[ld_all], inc="vv")
    P.barrier()

    def emit_ln(src_fn, ntiles, dst_sp, gvec=None, bvec=None):
        xa = AR_X[:, 0:4096].rearrange("p (c t) -> p c t", c=NCH)
        sq = AR_X[:, 4096:8192].rearrange("p (c t) -> p c t", c=NCH)
        xn = AR_X[:, 8192:12288].rearrange("p (c t) -> p c t", c=NCH)
        rows = S["rows"]
        mu, m2 = rows[:, 0:TT], rows[:, TT:2 * TT]
        va, rs = rows[:, 2 * TT:3 * TT], rows[:, 3 * TT:4 * TT]
        na = rows[:, 4 * TT:5 * TT]
        prev = {}
        for j in range(ntiles):
            t_ld = P.dma("sync", xa, src_fn(j),
                         waits=[prev.get("xa_done"), prev.get("st")])
            t_sq = P.op("scalar", lambda e: e.activation(
                sq[:, :, :], xa[:, :, :], ACT.Square),
                waits=[t_ld, prev.get("sq_done")], inc="aa")
            b6, w6 = acquire(6)
            b7, w7 = acquire(7)
            tmm = None
            for c in range(NCH):
                P.op("tensor", lambda e, c=c, b=b6: e.matmul(
                    b[0:1, :], S["ones_col"], xa[:, c, :],
                    start=(c == 0), stop=(c == NCH - 1)),
                    waits=([t_ld, t_on1] + w6) if c == 0 else [])
            for c in range(NCH):
                tmm = P.op("tensor", lambda e, c=c, b=b7: e.matmul(
                    b[0:1, :], S["ones_col"], sq[:, c, :],
                    start=(c == 0), stop=(c == NCH - 1)),
                    waits=([t_sq] + w7) if c == 0 else [],
                    inc="mm" if c == NCH - 1 else None)
            t1 = P.op("vector", lambda e, b=b6: e.tensor_scalar(
                mu, b[0:1, :], 1.0 / D, None, ALU.mult),
                waits=[tmm, prev.get("rows_done")], inc="vv")
            t2 = P.op("vector", lambda e, b=b7: e.tensor_scalar(
                m2, b[0:1, :], 1.0 / D, None, ALU.mult), inc="vv")
            P.wait("vector", [t1, t2])
            t3 = P.op("vector", lambda e: e.tensor_mul(va, mu, mu), inc="vv")
            P.wait("vector", [t3])
            t4 = P.op("vector", lambda e: e.tensor_sub(va, m2, va), inc="vv")
            P.wait("vector", [t4])
            t5 = P.op("vector", lambda e: e.tensor_scalar(
                va, va, LN_EPS, None, ALU.add), inc="vv")
            t6 = P.op("scalar", lambda e: e.activation(na, va, ACT.Sqrt),
                      waits=[t5], inc="aa")
            t7 = P.op("vector", lambda e: e.reciprocal(rs, na), waits=[t6],
                      inc="vv")
            release(6, t7)
            release(7, t7)
            b4, w4 = acquire(4)
            b5, w5 = acquire(5)
            t8 = P.op("tensor", lambda e, b=b4: e.matmul(
                b[:, :], S["ones_row"], mu, start=True, stop=True),
                waits=[t_on2, t1] + w4, inc="mm")
            t9 = P.op("tensor", lambda e, b=b5: e.matmul(
                b[:, :], S["ones_row"], rs, start=True, stop=True),
                waits=[t7] + w5, inc="mm")
            toks = []
            for c in range(NCH):
                ta = P.op("vector", lambda e, c=c, b=b4: e.tensor_sub(
                    xn[:, c, :], xa[:, c, :], b[:, :]),
                    waits=[t8, t9, prev.get("st")], inc="vv")
                P.wait("vector", [ta])
                tb = P.op("vector", lambda e, c=c, b=b5: e.tensor_mul(
                    xn[:, c, :], xn[:, c, :], b[:, :]), inc="vv")
                if gvec is not None:
                    P.wait("vector", [tb])
                    tb = P.op("vector", lambda e, c=c: e.tensor_scalar(
                        xn[:, c, :], xn[:, c, :], gvec[:, c:c + 1],
                        bvec[:, c:c + 1], ALU.mult, ALU.add), inc="vv")
                toks.append(tb)
            release(4, toks[-1])
            release(5, toks[-1])
            t_st = P.dma("gpsimd", dst_sp[:, :, j * TT:(j + 1) * TT],
                         xn[:, :, :], waits=toks)
            prev = {"xa_done": toks[-1], "st": t_st, "sq_done": tmm,
                    "rows_done": t9}
        P.barrier()

    def x_src(j):
        if j < NT_CTX:
            return x_ctx[:, :, j * TT:(j + 1) * TT]
        jo = j - NT_CTX
        return x_own[:, :, jo * TT:(jo + 1) * TT]

    emit_ln(x_src, NT_ALL, xn_sp)

    if PHASE_LIMIT >= 2:
        wq = AR_W[:, 0:24576].rearrange("p (c n) -> p c n", c=NCH)
        xb = AR_X[:, 0:4096].rearrange("p (c t) -> p c t", c=NCH)
        ev = AR_X[:, 4096:4608]
        mkv = AR_X[:, 4608:7680]
        t_w = P.dma("sync", wq, wqkv)
        t_mk = P.dma("sync", mkv, maskc)
        prev_ld = None
        for j in range(NT_ALL):
            own = j >= NT_CTX
            t_ld = P.dma("sync", xb, xn_sp[:, :, j * TT:(j + 1) * TT],
                         waits=[prev_ld])
            dcs = list(range(24)) if own else list(range(8, 24))
            last_st = None
            tmm = None
            for dc in dcs:
                bnk = dc % 8
                b, wb = acquire(bnk)
                for c in range(NCH):
                    tmm = P.op("tensor", lambda e, c=c, dc=dc, b=b: e.matmul(
                        b[:, :], wq[:, c, dc * 128:(dc + 1) * 128], xb[:, c, :],
                        start=(c == 0), stop=(c == NCH - 1)),
                        waits=([t_w, t_ld] + wb) if c == 0 else [],
                        inc="mm" if c == NCH - 1 else None)
                is_q = dc < 8
                bias = S["bq8"][:, dc:dc + 1] if is_q else S["bqkv"][:, dc:dc + 1]
                scale = 0.125 if is_q else 1.0
                tev = P.op("scalar", lambda e, b=b, bias=bias, scale=scale:
                           e.activation(ev, b[:, :], ACT.Identity, bias=bias,
                                        scale=scale),
                           waits=[tmm, t_bq8, last_st], inc="aa")
                if (not own) and (not is_q):
                    tev = P.op("vector", lambda e, j=j: e.tensor_mul(
                        ev, ev, mkv[:, j * TT:(j + 1) * TT]),
                        waits=[tev, t_mk], inc="vv")
                release(bnk, tev)
                if is_q:
                    jo = j - NT_CTX
                    dst = q_sp[:, dc, jo * TT:(jo + 1) * TT]
                elif dc < 16:
                    dst = k_sp[:, dc - 8, j * TT:(j + 1) * TT]
                else:
                    dst = v_sp[:, dc - 16, j * TT:(j + 1) * TT]
                last_st = P.dma("gpsimd", dst, ev, waits=[tev])
            prev_ld = tmm
        P.barrier()

        wg = AR_W[:, 0:8192].rearrange("p (c n) -> p c n", c=NCH)
        t_w = P.dma("sync", wg, wgate)
        prev_ld = None
        for jo in range(NT_OWN):
            j = NT_CTX + jo
            t_ld = P.dma("sync", xb, xn_sp[:, :, j * TT:(j + 1) * TT],
                         waits=[prev_ld])
            last_st = None
            tmm = None
            for dc in range(8):
                b, wb = acquire(dc)
                for c in range(NCH):
                    tmm = P.op("tensor", lambda e, c=c, dc=dc, b=b: e.matmul(
                        b[:, :], wg[:, c, dc * 128:(dc + 1) * 128], xb[:, c, :],
                        start=(c == 0), stop=(c == NCH - 1)),
                        waits=([t_w, t_ld] + wb) if c == 0 else [],
                        inc="mm" if c == NCH - 1 else None)
                tev = P.op("scalar", lambda e, b=b, dc=dc: e.activation(
                    ev, b[:, :], ACT.Sigmoid, bias=S["bgate"][:, dc:dc + 1],
                    scale=1.0), waits=[tmm, last_st], inc="aa")
                release(dc, tev)
                last_st = P.dma("gpsimd", gate_sp[:, dc, jo * TT:(jo + 1) * TT],
                                ev, waits=[tev])
            prev_ld = tmm
        P.barrier()

    if PHASE_LIMIT >= 3:
        qt = AR_W[:, 0:4096].rearrange("p (c t) -> p c t", c=NCH)
        kwin = AR_W[:, 4096:9728].rearrange("p (c t) -> p c t", c=NCH)
        kfar = [AR_W[:, 9728 + i * 4096:9728 + (i + 1) * 4096]
                .rearrange("p (c t) -> p c t", c=NCH) for i in range(2)]
        pbuf = [AR_W[:, 17920 + i * 4096:17920 + (i + 1) * 4096]
                .rearrange("p (c t) -> p c t", c=NCH) for i in range(2)]
        accb = AR_W[:, 26112:30208].rearrange("p (c t) -> p c t", c=NCH)
        gt_t = AR_X[:, 0:4096].rearrange("p (c t) -> p c t", c=NCH)
        att = AR_X[:, 4096:8192].rearrange("p (c t) -> p c t", c=NCH)
        sm1 = AR_X[:, 4096:5632].rearrange("p (s o h) -> p s o h", s=4, o=NOFF)
        tmpv2 = [AR_X[:, 8192:8704], AR_X[:, 11776:12288]]
        pfm_a = AR_X[:, 8704:9728]
        probs = AR_X[:, 9728:11264].rearrange("p (s o h) -> p s o h", s=4, o=NOFF)
        rdrow_buf = AR_X[0:16, 11264:11776]
        mrow = S["mrow"].rearrange("p (s h) -> p s h", s=4)
        pb_v = S["pb"].rearrange("p (o h) -> p o h", o=NOFF)
        far_offs = [(o, d) for o, d in enumerate(OFFS) if d > NEAR_MAX]

        for jo in range(NT_OWN):
            t0 = jo * TT
            kq0 = TCTX + t0
            t_q = P.dma("sync", qt, q_sp[:, :, t0:t0 + TT])
            t_nw = P.dma("sync", kwin, k_sp[:, :, kq0 - NEAR_MAX:kq0 + TT])
            sbank = []
            for s in range(4):
                b, wb = acquire(s)
                sbank.append(b)
                P.wait("tensor", wb)
            prod_toks = {}
            far_ld = {}
            sel_last = None
            for o, dlt in enumerate(OFFS):
                pslot = pbuf[o % 2]
                if dlt <= NEAR_MAX:
                    ksrc = kwin[:, :, NEAR_MAX - dlt:NEAR_MAX - dlt + TT]
                    kw = t_nw
                else:
                    fi = far_offs.index((o, dlt))
                    slot = kfar[fi % 2]
                    freew = prod_toks.get(far_offs[fi - 2][0]) if fi >= 2 else None
                    far_ld[fi] = P.dma("sync", slot,
                                       k_sp[:, :, kq0 - dlt:kq0 - dlt + TT],
                                       waits=[freew])
                    ksrc = slot
                    kw = far_ld[fi]
                tp = None
                for c in range(NCH):
                    tp = P.op("vector", lambda e, c=c, ksrc=ksrc, pslot=pslot:
                              e.tensor_mul(pslot[:, c, :], qt[:, c, :],
                                           ksrc[:, c, :]),
                              waits=[t_q, kw, prod_toks.get(o - 2)]
                              if c == 0 else [], inc="vv" if c == NCH - 1
                              else None)
                for s in range(4):
                    for c in range(NCH):
                        sel_last = P.op("tensor",
                            lambda e, s=s, c=c, o=o, pslot=pslot: e.matmul(
                                sbank[s][:, o * 16:(o + 1) * 16],
                                pslot[:, c, s * 128:(s + 1) * 128],
                                S["sel"][:, c * 16:(c + 1) * 16],
                                start=(c == 0), stop=(c == NCH - 1)),
                            waits=[tp] if c == 0 else [],
                            inc="mm" if c == NCH - 1 else None)
                prod_toks[o] = sel_last
            sf_done = []
            rd_done = []
            for s in range(4):
                t1 = P.op("vector", lambda e, s=s: e.tensor_add(
                    sm1[:, s, :, :],
                    sbank[s][:, 0:NOFF * 16].rearrange("p (o h) -> p o h",
                                                       o=NOFF), pb_v),
                    waits=[prod_toks[NOFF - 1]], inc="vv")
                t4 = P.op("scalar", lambda e, s=s: e.activation(
                    probs[:, s, :, :], sm1[:, s, :, :], ACT.Exp),
                    waits=[t1], inc="aa")
                t5 = P.op("vector", lambda e, s=s: e.reduce_sum(
                    mrow[:, s, :],
                    probs[:, s, :, :].rearrange("p o h -> p h o"), axis=AXX),
                    waits=[t4], inc="vv")
                P.wait("vector", [t5])
                t6 = P.op("vector", lambda e, s=s: e.reciprocal(
                    S["st16"][:, s * 16:(s + 1) * 16], mrow[:, s, :]),
                    inc="vv")
                sf_done.append(t4)
                rd_done.append(t6)
            b7r, w7r = acquire(7)
            ttr = None
            for s in range(4):
                ttr = P.op("tensor", lambda e, s=s, b=b7r: e.transpose(
                    b[0:16, s * 128:(s + 1) * 128],
                    S["st16"][:, s * 16:(s + 1) * 16], S["iden"]),
                    waits=([rd_done[-1], t_idn] + w7r) if s == 0 else [],
                    inc="mm" if s == 3 else None)
            t_rd = P.op("vector", lambda e, b=b7r: e.tensor_copy(
                rdrow_buf[0:16, :], b[0:16, 0:TT]), waits=[ttr], inc="vv")
            release(7, t_rd)
            if DEBUG:
                P.dma("gpsimd", probs_dbg[:, jo * 4:(jo + 1) * 4, :, :],
                      probs[:, :, :, :], waits=sf_done)
            t_ac = P.op("vector", lambda e: e.memset(accb[:, :, :], 0.0),
                        waits=[sf_done[-1]], inc="vv")
            t_vw = P.dma("sync", kwin, v_sp[:, :, kq0 - NEAR_MAX:kq0 + TT],
                         waits=[prod_toks[NOFF - 1]])
            mac_toks = {}
            farv_ld = {}
            for o, dlt in enumerate(OFFS):
                pslot = pfm_a[0:16, (o % 2) * TT:(o % 2) * TT + TT]
                b6, w6 = acquire(6)
                ttr = None
                for s in range(4):
                    ttr = P.op("tensor", lambda e, s=s, o=o, b=b6: e.transpose(
                        b[0:16, s * 128:(s + 1) * 128], probs[:, s, o, :],
                        S["iden"]),
                        waits=([sf_done[-1], t_idn] + w6) if s == 0 else [],
                        inc="mm" if s == 3 else None)
                tcp = P.op("vector", lambda e, b=b6, pslot=pslot: e.tensor_copy(
                    pslot[0:16, :], b[0:16, 0:TT]),
                    waits=[ttr, mac_toks.get(o - 2)], inc="vv")
                release(6, tcp)
                if dlt <= NEAR_MAX:
                    vsrc = kwin[:, :, NEAR_MAX - dlt:NEAR_MAX - dlt + TT]
                    vw = t_vw
                else:
                    fi = far_offs.index((o, dlt))
                    freew = mac_toks.get(far_offs[fi - 2][0]) if fi >= 2 else None
                    farv_ld[fi] = P.dma("sync", kfar[fi % 2],
                                        v_sp[:, :, kq0 - dlt:kq0 - dlt + TT],
                                        waits=[freew, far_ld.get(fi)])
                    vsrc = kfar[fi % 2]
                    vw = farv_ld[fi]
                last = None
                for c in range(NCH):
                    bnk = 4 + (c % 2)
                    tv = tmpv2[c % 2]
                    b, wb = acquire(bnk)
                    tb = P.op("tensor", lambda e, c=c, b=b, pslot=pslot:
                              e.matmul(b[:, :], S["bc"][:, c * 128:(c + 1) * 128],
                                       pslot[0:16, :], start=True, stop=True),
                              waits=[tcp] + wb, inc="mm")
                    tm = P.op("vector", lambda e, c=c, b=b, vsrc=vsrc, tv=tv:
                              e.tensor_mul(tv, vsrc[:, c, :], b[:, :]),
                              waits=[tb, vw, t_ac], inc="vv")
                    P.wait("vector", [tm])
                    last = P.op("vector", lambda e, c=c, tv=tv: e.tensor_add(
                        accb[:, c, :], accb[:, c, :], tv), inc="vv")
                    release(bnk, last)
                mac_toks[o] = last
            t_g = P.dma("sync", gt_t, gate_sp[:, :, t0:t0 + TT])
            gm = None
            for c in range(NCH):
                bnk = 4 + (c % 2)
                b, wb = acquire(bnk)
                tb = P.op("tensor", lambda e, c=c, b=b: e.matmul(
                    b[:, :], S["bc"][:, c * 128:(c + 1) * 128],
                    rdrow_buf[0:16, :], start=True, stop=True),
                    waits=[t_rd] + wb, inc="mm")
                tv = tmpv2[c % 2]
                tm = P.op("vector", lambda e, c=c, b=b, tv=tv: e.tensor_mul(
                    tv, gt_t[:, c, :], b[:, :]),
                    waits=[tb, t_g, mac_toks[NOFF - 1]], inc="vv")
                P.wait("vector", [tm])
                gm = P.op("vector", lambda e, c=c, tv=tv: e.tensor_mul(
                    accb[:, c, :], accb[:, c, :], tv), inc="vv")
                release(bnk, gm)
            wo = AR_W[:, 4096:12288].rearrange("p (c n) -> p c n", c=NCH)
            t_wo = P.dma("sync", wo, wout,
                         waits=[prod_toks[NOFF - 1], mac_toks[NOFF - 1]])
            pr_done = []
            for dc in range(8):
                b, wb = acquire(dc)
                tmm = None
                for c in range(NCH):
                    tmm = P.op("tensor", lambda e, c=c, dc=dc, b=b: e.matmul(
                        b[:, :], wo[:, c, dc * 128:(dc + 1) * 128],
                        accb[:, c, :], start=(c == 0), stop=(c == NCH - 1)),
                        waits=([t_wo, gm] + wb) if c == 0 else [],
                        inc="mm" if c == NCH - 1 else None)
                tev = P.op("scalar", lambda e, b=b, dc=dc: e.activation(
                    att[:, dc, :], b[:, :], ACT.Identity,
                    bias=S["bout"][:, dc:dc + 1], scale=1.0),
                    waits=[tmm], inc="aa")
                release(dc, tev)
                pr_done.append(tev)
            P.dma("gpsimd", attn_sp[:, :, t0:t0 + TT], att[:, :, :],
                  waits=pr_done)
            P.barrier()

    if PHASE_LIMIT >= 4:
        emit_ln(lambda j: xn_sp[:, :, TCTX + j * TT:TCTX + (j + 1) * TT],
                NT_OWN, xn2_sp, gvec=S["lng"], bvec=S["lnb"])

    if PHASE_LIMIT >= 5:
        w1s = [AR_W[:, i * 8192:(i + 1) * 8192]
               .rearrange("p (c n) -> p c n", c=NCH) for i in range(2)]
        w2s = [AR_W[:, 16384 + i * 8192:16384 + (i + 1) * 8192]
               .rearrange("p (c n) -> p c n", c=NCH) for i in range(2)]
        xb2 = AR_X[:, 0:4096].rearrange("p (c t) -> p c t", c=NCH)
        hg = AR_X[:, 4096:8192].rearrange("p (c t) -> p c t", c=NCH)
        abacc = AR_X[:, 8192:12288].rearrange("p (c t) -> p c t", c=NCH)
        prev_tile = {}
        for jo in range(NT_OWN):
            t_ld = P.dma("sync", xb2, xn2_sp[:, :, jo * TT:(jo + 1) * TT],
                         waits=[prev_tile.get("xb2")])
            w_toks = {}
            gel_done = {}
            acc_done = None
            for g in range(4):
                sl = g % 2
                w_toks[("w1", g)] = P.dma(
                    "sync", w1s[sl], aw1[:, :, g * 1024:(g + 1) * 1024],
                    waits=[gel_done.get(g - 2), prev_tile.get("w1")])
                w_toks[("w2", g)] = P.dma(
                    "sync", w2s[sl], aw2[:, g * 8:(g + 1) * 8, :],
                    waits=[acc_done if g >= 2 else None, prev_tile.get("w2")])
                tg = None
                for i in range(8):
                    dc = g * 8 + i
                    bnk = i % 4
                    b, wb = acquire(bnk)
                    tmm = None
                    for c in range(NCH):
                        tmm = P.op("tensor", lambda e, c=c, i=i, b=b, sl=sl:
                                   e.matmul(b[:, :],
                                            w1s[sl][:, c, i * 128:(i + 1) * 128],
                                            xb2[:, c, :], start=(c == 0),
                                            stop=(c == NCH - 1)),
                                   waits=([w_toks[("w1", g)], t_ld] + wb)
                                   if c == 0 else [],
                                   inc="mm" if c == NCH - 1 else None)
                    tg = P.op("scalar", lambda e, b=b, dc=dc, i=i: e.activation(
                        hg[:, i, :], b[:, :], ACT.Gelu,
                        bias=S["ab1"][:, dc:dc + 1], scale=1.0),
                        waits=[tmm, gel_done.get(g - 1)], inc="aa")
                    release(bnk, tg)
                gel_done[g] = tg
                for half in range(2):
                    for i in range(4):
                        dc = half * 4 + i
                        bnk = 4 + i
                        b, wb = acquire(bnk)
                        tmm = None
                        for c in range(NCH):
                            tmm = P.op("tensor",
                                lambda e, c=c, dc=dc, b=b, sl=sl: e.matmul(
                                    b[:, :],
                                    w2s[sl][:, c, dc * 128:(dc + 1) * 128],
                                    hg[:, c, :], start=(c == 0),
                                    stop=(c == NCH - 1)),
                                waits=([w_toks[("w2", g)], tg] + wb)
                                if c == 0 else [],
                                inc="mm" if c == NCH - 1 else None)
                        if g == 0:
                            tev = P.op("scalar", lambda e, b=b, dc=dc:
                                       e.activation(abacc[:, dc, :], b[:, :],
                                                    ACT.Identity,
                                                    bias=S["ab2"][:, dc:dc + 1],
                                                    scale=1.0),
                                       waits=[tmm], inc="aa")
                        else:
                            tev = P.op("vector", lambda e, b=b, dc=dc:
                                       e.tensor_add(abacc[:, dc, :],
                                                    abacc[:, dc, :], b[:, :]),
                                       waits=[tmm], inc="vv")
                        release(bnk, tev)
                        acc_done = tev
            t_st = P.dma("gpsimd", abel_sp[:, :, jo * TT:(jo + 1) * TT],
                         abacc[:, :, :], waits=[acc_done])
            prev_tile = {"xb2": tmm, "w1": tmm, "w2": tmm, "st": t_st}
            P.wait("sync", [t_st])
        P.barrier()

    if PHASE_LIMIT >= 6:
        gw = AR_W[:, 0:NCH * E_EXP].rearrange("p (c n) -> p c n", c=NCH)
        xb2 = AR_X[:, 0:4096].rearrange("p (c t) -> p c t", c=NCH)
        lt = S["tmp6"][:, 0:E_EXP]
        k6 = S["tmp6"][:, E_EXP:2 * E_EXP]
        m1 = S["st16"][:, 0:1]
        nm1 = S["st16"][:, 1:2]
        m2 = S["st16"][:, 2:3]
        den = S["st16"][:, 3:4]
        rin = S["st16"][:, 4:5]
        t_w = P.dma("sync", gw, gatew)
        prev_ld = None
        for jo in range(NT_OWN):
            t_ld = P.dma("sync", xb2, xn2_sp[:, :, jo * TT:(jo + 1) * TT],
                         waits=[prev_ld])
            for s in range(4):
                st = jo * 4 + s
                b, wb = acquire(s % 2)
                tmm = None
                for c in range(NCH):
                    tmm = P.op("tensor", lambda e, c=c, s=s, b=b: e.matmul(
                        b[:, 0:E_EXP], xb2[:, c, s * 128:(s + 1) * 128],
                        gw[:, c, :], start=(c == 0), stop=(c == NCH - 1)),
                        waits=([t_w, t_ld] + wb) if c == 0 else [],
                        inc="mm" if c == NCH - 1 else None)
                ta = P.op("vector", lambda e, b=b: e.tensor_add(
                    lt, b[:, 0:E_EXP], S["gateb"]), waits=[tmm], inc="vv")
                P.wait("vector", [ta])
                tb = P.op("vector", lambda e: e.reduce_max(m1, lt, axis=AXX),
                          inc="vv")
                P.wait("vector", [tb])
                tc = P.op("vector", lambda e: e.tensor_scalar(
                    nm1, m1, -1.0, None, ALU.mult), inc="vv")
                td = P.op("vector", lambda e: e.tensor_scalar(
                    k6, lt, m1[:, 0:1], None, ALU.is_ge), inc="vv")
                P.wait("vector", [tc, td])
                te = P.op("vector", lambda e: e.scalar_tensor_tensor(
                    k6, k6, -1e30, lt, ALU.mult, ALU.add), inc="vv")
                P.wait("vector", [te])
                tf = P.op("vector", lambda e: e.reduce_max(m2, k6, axis=AXX),
                          inc="vv")
                tex = P.op("scalar", lambda e: e.activation(
                    k6, lt, ACT.Exp, bias=nm1[:, 0:1], scale=1.0),
                    waits=[tc, tf], inc="aa")
                P.wait("vector", [tf, tex])
                tg2 = P.op("vector", lambda e: e.tensor_scalar(
                    lt, lt, m2[:, 0:1], None, ALU.is_ge), inc="vv")
                P.wait("vector", [tg2])
                th = P.op("vector", lambda e: e.tensor_mul(k6, k6, lt),
                          inc="vv")
                P.wait("vector", [th])
                ti = P.op("vector", lambda e: e.reduce_sum(den, k6, axis=AXX),
                          inc="vv")
                P.wait("vector", [ti])
                tj = P.op("vector", lambda e: e.reciprocal(rin, den), inc="vv")
                P.wait("vector", [tj])
                tk = P.op("vector", lambda e, st=st: e.tensor_scalar(
                    S["wtok"][:, st * E_EXP:(st + 1) * E_EXP], k6, rin[:, 0:1],
                    None, ALU.mult), inc="vv")
                b6, w6 = acquire(6)
                tt = P.op("tensor", lambda e, st=st, b=b6: e.transpose(
                    b[0:E_EXP, 0:128], S["wtok"][:, st * E_EXP:(st + 1) * E_EXP],
                    S["iden"]), waits=[tk, t_idn] + w6, inc="mm")
                tv = P.op("vector", lambda e, st=st, b=b6: e.tensor_copy(
                    S["wfm"][0:E_EXP, st * 128:(st + 1) * 128],
                    b[0:E_EXP, 0:128]), waits=[tt], inc="vv")
                release(6, tv)
                release(s % 2, ta)
            prev_ld = tmm
        if DEBUG:
            P.dma("gpsimd", wtok_dbg, S["wtok"], waits=[("vv", P.cnt["vv"])])
        P.barrier()

    if PHASE_LIMIT >= 7:
        w1v = AR_W[:, 0:16384].rearrange("p (c n) -> p c n", c=NCH)
        w2v = AR_W[:, 16384:32768].rearrange("p (c n) -> p c n", c=NHC)
        xb2 = AR_X[:, 0:4096].rearrange("p (c t) -> p c t", c=NCH)
        hh = AR_X[:, 4096:8192].rearrange("p (c t) -> p c t", c=NCH)
        mac = AR_X[:, 8192:12288].rearrange("p (c t) -> p c t", c=NCH)
        prev = {}
        for jo in range(NT_OWN):
            t_ld = P.dma("sync", xb2, xn2_sp[:, :, jo * TT:(jo + 1) * TT],
                         waits=[prev.get("xb2")])
            eo_done = None
            h_done = None
            wprev1 = prev.get("w1")
            wprev2 = prev.get("w2")
            for ei in range(E_EXP):
                t_w1 = P.dma("sync", w1v, ew1[ei], waits=[h_done, wprev1])
                t_w2 = P.dma("sync", w2v, ew2[ei], waits=[eo_done, wprev2])
                wprev1, wprev2 = t_w1, t_w2
                b6, w6 = acquire(6)
                wrow = S["rows"][:, 4 * TT:5 * TT]
                ttw = None
                for s_ in range(4):
                    ttw = P.op("tensor", lambda e, ei=ei, jo=jo, s_=s_, b=b6:
                               e.transpose(b[0:1, s_ * 128:(s_ + 1) * 128],
                                           S["wtok"][:, (jo * 4 + s_) * E_EXP
                                                     + ei:(jo * 4 + s_) *
                                                     E_EXP + ei + 1],
                                           S["iden"]),
                               waits=([t_idn] + w6) if s_ == 0 else [],
                               inc="mm" if s_ == 3 else None)
                tcw = P.op("vector", lambda e, b=b6: e.tensor_copy(
                    wrow, b[0:1, 0:TT]), waits=[ttw, prev.get("wrow")],
                    inc="vv")
                prev["wrow"] = None
                twb = P.op("tensor", lambda e, b=b6: e.matmul(
                    b[:, :], S["ones_row"], wrow,
                    start=True, stop=True), waits=[t_on2, tcw], inc="mm")
                prev["wrow"] = twb
                last_eo_prev = eo_done
                for half in range(2):
                    h_done = None
                    for hc in range(NCH):
                        gc_ = half * NCH + hc
                        bnk = hc % 4
                        b, wb = acquire(bnk)
                        tmm = None
                        for c in range(NCH):
                            tmm = P.op("tensor",
                                lambda e, c=c, gc_=gc_, b=b: e.matmul(
                                    b[:, :],
                                    w1v[:, c, gc_ * 128:(gc_ + 1) * 128],
                                    xb2[:, c, :], start=(c == 0),
                                    stop=(c == NCH - 1)),
                                waits=([t_w1, t_ld] + wb) if c == 0 else [],
                                inc="mm" if c == NCH - 1 else None)
                        tg = P.op("scalar", lambda e, b=b, ei=ei, gc_=gc_,
                                  hc=hc: e.activation(
                                      hh[:, hc, :], b[:, :], ACT.Gelu,
                                      bias=S["eb1"][:, ei * NHC + gc_:
                                                    ei * NHC + gc_ + 1],
                                      scale=1.0),
                                  waits=[tmm, prev.get(("h", hc)),
                                         last_eo_prev], inc="aa")
                        tsc = P.op("vector", lambda e, hc=hc, b=b6:
                                   e.tensor_mul(hh[:, hc, :], hh[:, hc, :],
                                                b[:, :]),
                                   waits=[tg, twb], inc="vv")
                        release(bnk, tsc)
                        prev[("h", hc)] = tsc
                        h_done = tsc
                    for dc in range(NCH):
                        bnk = (4, 5, 7)[dc % 3]
                        b, wb = acquire(bnk)
                        tmm = None
                        first = True
                        if ei == 0 and half == 0:
                            tmm = P.op("tensor", lambda e, dc=dc, jo=jo, b=b:
                                       e.matmul(b[:, :],
                                                S["eb2"][0:E_EXP,
                                                         dc * 128:(dc + 1) * 128],
                                                S["wfm"][0:E_EXP,
                                                         jo * TT:(jo + 1) * TT],
                                                start=True, stop=False),
                                       waits=[ld_all, t_ld] + wb, inc="mm")
                            first = False
                        for c in range(NCH):
                            kk = half * NCH + c
                            tmm = P.op("tensor", lambda e, c=c, kk=kk, dc=dc,
                                       b=b, first=(first and c == 0): e.matmul(
                                           b[:, :],
                                           w2v[:, kk, dc * 128:(dc + 1) * 128],
                                           hh[:, c, :], start=first,
                                           stop=(c == NCH - 1)),
                                       waits=([t_w2, h_done] + wb)
                                       if c == 0 else [],
                                       inc="mm" if c == NCH - 1 else None)
                        if ei == 0 and half == 0:
                            tev = P.op("vector", lambda e, b=b, dc=dc:
                                       e.tensor_copy(mac[:, dc, :], b[:, :]),
                                       waits=[tmm, prev.get(("m", dc))],
                                       inc="vv")
                        else:
                            tev = P.op("vector", lambda e, b=b, dc=dc:
                                       e.tensor_add(mac[:, dc, :],
                                                    mac[:, dc, :], b[:, :]),
                                       waits=[tmm, prev.get(("m", dc))],
                                       inc="vv")
                        release(bnk, tev)
                        eo_done = tev
                        prev[("m", dc)] = tev
                release(6, h_done)
            t_st = P.dma("gpsimd", macc_sp[:, :, jo * TT:(jo + 1) * TT],
                         mac[:, :, :], waits=[eo_done])
            prev["xb2"] = tmm
            prev["w1"] = t_w1
            prev["w2"] = t_w2
            P.wait("sync", [t_st])
            P.barrier()
        P.barrier()

    if PHASE_LIMIT >= 8:
        sA = AR_W[:, 0:8192].rearrange("p (c n) -> p c n", c=NCH)
        sB = AR_W[:, 8192:16384].rearrange("p (c n) -> p c n", c=NCH)
        sB2 = AR_W[:, 16384:24576].rearrange("p (c n) -> p c n", c=NCH)
        sB3 = AR_W[:, 24576:32768].rearrange("p (c n) -> p c n", c=NCH)
        sY = AR_X[:, 0:8192].rearrange("p (c n) -> p c n", c=NCH)

        def mat_mm(lhs_v, rhs_v, out_v, pre_waits, tag_inplace=False,
                   coeffs=None, iterm=None):
            done = []
            for jh in range(2):
                mm_last = None
                for ic in range(NCH):
                    b, wb = acquire(ic)
                    for kc in range(NCH):
                        mm_last = P.op("tensor",
                            lambda e, ic=ic, kc=kc, jh=jh, b=b: e.matmul(
                                b[:, :], lhs_v[:, kc, ic * 128:(ic + 1) * 128],
                                rhs_v[:, kc, jh * 512:(jh + 1) * 512],
                                start=(kc == 0), stop=(kc == NCH - 1)),
                            waits=(pre_waits + wb) if kc == 0 else [],
                            inc="mm" if kc == NCH - 1 else None)
                if coeffs is None:
                    for ic in range(NCH):
                        tev = P.op("vector", lambda e, ic=ic, jh=jh, b=ps[ic]:
                                   e.tensor_copy(
                                       out_v[:, ic, jh * 512:(jh + 1) * 512],
                                       b[:, :]), waits=[mm_last], inc="vv")
                        release(ic, tev)
                        done.append(tev)
                else:
                    c1, c2, c3 = coeffs
                    p1s = []
                    for ic in range(NCH):
                        tp1 = P.op("vector", lambda e, ic=ic, jh=jh, b=ps[ic],
                                   c1=c1: e.scalar_tensor_tensor(
                                       out_v[:, ic, jh * 512:(jh + 1) * 512],
                                       sB[:, ic, jh * 512:(jh + 1) * 512], c1,
                                       b[:, :], ALU.mult, ALU.add),
                                   waits=[mm_last], inc="vv")
                        release(ic, tp1)
                        p1s.append(tp1)
                    P.wait("vector", [p1s[-1]])
                    p2s = []
                    for ic in range(NCH):
                        tp2 = P.op("vector", lambda e, ic=ic, jh=jh, c2=c2:
                                   e.scalar_tensor_tensor(
                                       out_v[:, ic, jh * 512:(jh + 1) * 512],
                                       sB2[:, ic, jh * 512:(jh + 1) * 512], c2,
                                       out_v[:, ic, jh * 512:(jh + 1) * 512],
                                       ALU.mult, ALU.add), inc="vv")
                        p2s.append(tp2)
                    P.wait("vector", [p2s[-1]])
                    for ic in range(NCH):
                        tp3 = P.op("vector", lambda e, ic=ic, jh=jh, c3=c3:
                                   e.scalar_tensor_tensor(
                                       out_v[:, ic, jh * 512:(jh + 1) * 512],
                                       sB3[:, ic, jh * 512:(jh + 1) * 512], c3,
                                       out_v[:, ic, jh * 512:(jh + 1) * 512],
                                       ALU.mult, ALU.add), inc="vv")
                        done.append(tp3)
                    if iterm is not None:
                        P.wait("vector", [done[-1]])
                        for ic in range(NCH):
                            col = ic * 128
                            if col // 512 != jh:
                                continue
                            tpi = P.op("vector", lambda e, ic=ic, col=col,
                                       c0=iterm: e.scalar_tensor_tensor(
                                           out_v[:, ic, col:col + 128],
                                           S["iden"], c0,
                                           out_v[:, ic, col:col + 128],
                                           ALU.mult, ALU.add), inc="vv")
                            done.append(tpi)
            return done[-1]

        for gi in range(4):
            cs = list(EXPM_A)
            if gi == 3:
                cs = [a * ((-1.0) ** k) for k, a in enumerate(cs)]
            t_a = P.dma("sync", sA, gens[gi])
            t_neg = P.op("vector", lambda e: e.tensor_scalar(
                sB[:, :, :], sA[:, :, :], -1.0, None, ALU.mult),
                waits=[t_a], inc="vv")
            tk2 = mat_mm(sA, sB, sB2, [t_a, t_neg])
            tk3 = mat_mm(sA, sB2, sB3, [tk2])
            tk4 = mat_mm(sA, sB3, sY, [tk3])
            mv = []
            for ic in range(NCH):
                tmv = P.op("vector", lambda e, ic=ic: e.tensor_copy(
                    sA[:, ic, :], sY[:, ic, :]), waits=[tk4], inc="vv")
                mv.append(tmv)
            t_b4 = mv[-1]
            y0 = []
            for ic in range(NCH):
                ty = P.op("vector", lambda e, ic=ic, c13=cs[13]:
                          e.tensor_scalar(sY[:, ic, :], sB[:, ic, :], c13,
                                          None, ALU.mult),
                          waits=[t_b4], inc="vv")
                y0.append(ty)
            P.wait("vector", [y0[-1]])
            y0b = []
            for ic in range(NCH):
                ty = P.op("vector", lambda e, ic=ic, c14=cs[14]:
                          e.scalar_tensor_tensor(sY[:, ic, :], sB2[:, ic, :],
                                                 c14, sY[:, ic, :], ALU.mult,
                                                 ALU.add), inc="vv")
                y0b.append(ty)
            P.wait("vector", [y0b[-1]])
            yi = []
            for ic in range(NCH):
                ty = P.op("vector", lambda e, ic=ic, c12=cs[12]:
                          e.scalar_tensor_tensor(
                              sY[:, ic, ic * 128:(ic + 1) * 128], S["iden"],
                              c12, sY[:, ic, ic * 128:(ic + 1) * 128],
                              ALU.mult, ALU.add), waits=[t_idn], inc="vv")
                yi.append(ty)
            t_y = yi[-1]
            for g in (8, 4, 0):
                t_y = mat_mm(sA, sY, sY, [t_y],
                             coeffs=(cs[g + 1], cs[g + 2], cs[g + 3]),
                             iterm=cs[g])
            P.dma("gpsimd", expm_sp[gi], sY[:, :, :], waits=[t_y])
            P.barrier()
        lhs = AR_W[:, 0:8192].rearrange("p (c n) -> p c n", c=NCH)
        rhs = AR_W[:, 8192:16384].rearrange("p (c n) -> p c n", c=NCH)
        outm = AR_W[:, 16384:24576].rearrange("p (c n) -> p c n", c=NCH)
        t_l = P.dma("sync", lhs, expm_sp[2])
        t_r = P.dma("sync", rhs, expm_sp[3])
        t_p1 = mat_mm(lhs, rhs, outm, [t_l, t_r])
        t_l = P.dma("sync", lhs, expm_sp[1], waits=[t_p1])
        cp = []
        for ic in range(NCH):
            tc2 = P.op("vector", lambda e, ic=ic: e.tensor_copy(
                rhs[:, ic, :], outm[:, ic, :]), waits=[t_p1], inc="vv")
            cp.append(tc2)
        t_p2 = mat_mm(lhs, rhs, outm, [t_l, cp[-1]])
        t_l = P.dma("sync", lhs, expm_sp[0], waits=[t_p2])
        cp = []
        for ic in range(NCH):
            tc2 = P.op("vector", lambda e, ic=ic: e.tensor_copy(
                rhs[:, ic, :], outm[:, ic, :]), waits=[t_p2], inc="vv")
            cp.append(tc2)
        t_rm = mat_mm(lhs, rhs, outm, [t_l, cp[-1]])
        P.barrier()

    if PHASE_LIMIT >= 9:
        rv = AR_W[:, 16384:24576].rearrange("p (c n) -> p c n", c=NCH)
        mt = AR_X[:, 0:4096].rearrange("p (c t) -> p c t", c=NCH)
        nt = AR_X[:, 4096:8192].rearrange("p (c t) -> p c t", c=NCH)
        prev_t = {}
        for jo in range(NT_OWN):
            t_ld = P.dma("sync", mt, macc_sp[:, :, jo * TT:(jo + 1) * TT],
                         waits=[prev_t.get("mm")])
            ev_done = []
            tmm = None
            for dc in range(NCH):
                b, wb = acquire(dc)
                for c in range(NCH):
                    tmm = P.op("tensor", lambda e, c=c, dc=dc, b=b: e.matmul(
                        b[:, :], rv[:, c, dc * 128:(dc + 1) * 128],
                        mt[:, c, :], start=(c == 0), stop=(c == NCH - 1)),
                        waits=([t_ld] + wb) if c == 0 else [],
                        inc="mm" if c == NCH - 1 else None)
                tev = P.op("vector", lambda e, b=b, dc=dc: e.tensor_copy(
                    nt[:, dc, :], b[:, :]),
                    waits=[tmm, prev_t.get("st")], inc="vv")
                release(dc, tev)
                ev_done.append(tev)
            t_st = P.dma("gpsimd", nona_sp[:, :, jo * TT:(jo + 1) * TT],
                         nt[:, :, :], waits=ev_done)
            prev_t = {"mm": tmm, "st": t_st}
        P.barrier()

    if PHASE_LIMIT >= 10:
        ab_v = AR_X[:, 0:4096].rearrange("p (c t) -> p c t", c=NCH)
        no_v = AR_X[:, 4096:8192].rearrange("p (c t) -> p c t", c=NCH)
        tmpd = AR_X[:, 8192:8704]
        xn_v = AR_W[:, 0:4096].rearrange("p (c t) -> p c t", c=NCH)
        at_v = AR_W[:, 4096:8192].rearrange("p (c t) -> p c t", c=NCH)
        x_v = AR_W[:, 8192:12288].rearrange("p (c t) -> p c t", c=NCH)
        ou_v = AR_W[:, 12288:16384].rearrange("p (c t) -> p c t", c=NCH)
        grow = S["rows"][:, 0:TT]
        prev_t = {}
        for jo in range(NT_OWN):
            sl = slice(jo * TT, (jo + 1) * TT)
            t_ab = P.dma("sync", ab_v, abel_sp[:, :, sl], waits=[prev_t.get("d")])
            t_no = P.dma("sync", no_v, nona_sp[:, :, sl])
            t_xn = P.dma("sync", xn_v, xn_sp[:, :, TCTX + jo * TT:
                                             TCTX + (jo + 1) * TT])
            t_at = P.dma("sync", at_v, attn_sp[:, :, sl])
            t_x = P.dma("sync", x_v, x_own[:, :, sl])
            b6, w6 = acquire(6)
            tmm = None
            for i in range(16):
                src = ab_v if i < 8 else no_v
                cc = i % 8
                tmm = P.op("tensor", lambda e, i=i, src=src, cc=cc, b=b6:
                           e.matmul(b[0:1, :], S["gcw"][:, i:i + 1],
                                    src[:, cc, :], start=(i == 0),
                                    stop=(i == 15)),
                           waits=([t_ab, t_no, ld_all] + w6) if i == 0 else [],
                           inc="mm" if i == 15 else None)
            tsg = P.op("scalar", lambda e, b=b6: e.activation(
                grow, b[0:1, :], ACT.Sigmoid, bias=S["gcb"][0:1, 0:1],
                scale=1.0), waits=[tmm, prev_t.get("gb")], inc="aa")
            release(6, tsg)
            b7, w7 = acquire(7)
            tgb = P.op("tensor", lambda e, b=b7: e.matmul(
                b[:, :], S["ones_row"], grow, start=True, stop=True),
                waits=[tsg, t_on2] + w7, inc="mm")
            prev_t["gb"] = tgb
            outs = []
            dlast = None
            for c in range(NCH):
                td = P.op("vector", lambda e, c=c: e.tensor_sub(
                    tmpd, no_v[:, c, :], ab_v[:, c, :]),
                    waits=[t_ab, t_no, tgb, dlast], inc="vv")
                P.wait("vector", [td])
                tm2 = P.op("vector", lambda e, c=c, b=b7: e.tensor_mul(
                    tmpd, tmpd, b[:, :]), inc="vv")
                P.wait("vector", [tm2])
                t1 = P.op("vector", lambda e, c=c: e.tensor_add(
                    ou_v[:, c, :], ab_v[:, c, :], tmpd), inc="vv")
                P.wait("vector", [t1])
                t2 = P.op("vector", lambda e, c=c: e.tensor_add(
                    ou_v[:, c, :], ou_v[:, c, :], x_v[:, c, :]),
                    waits=[t_x], inc="vv")
                P.wait("vector", [t2])
                t3 = P.op("vector", lambda e, c=c: e.tensor_add(
                    ou_v[:, c, :], ou_v[:, c, :], xn_v[:, c, :]),
                    waits=[t_xn], inc="vv")
                P.wait("vector", [t3])
                t4 = P.op("vector", lambda e, c=c: e.tensor_add(
                    ou_v[:, c, :], ou_v[:, c, :], at_v[:, c, :]),
                    waits=[t_at], inc="vv")
                dlast = t4
                outs.append(t4)
            release(7, dlast)
            t_st = P.dma("gpsimd", out_t[:, :, sl], ou_v[:, :, :], waits=outs)
            prev_t["d"] = dlast
            prev_t["st"] = t_st
            P.wait("sync", [t_st])
        P.wait("gpsimd", [("gd", P.cnt["gd"])])

    steps = P.steps
    with nc.Block() as block:
        @block.sync
        def _(eng):
            for f in steps["sync"]:
                f(eng)

        @block.tensor
        def _(eng):
            for f in steps["tensor"]:
                f(eng)

        @block.vector
        def _(eng):
            for f in steps["vector"]:
                f(eng)

        @block.scalar
        def _(eng):
            for f in steps["scalar"]:
                f(eng)

        @block.gpsimd
        def _(eng):
            for f in steps["gpsimd"]:
                f(eng)

    stack.close()
    return nc


def host_inputs(inputs):
    x = np.asarray(inputs["L0_attn_x"], dtype=np.float32)

    def pm(w2d, nch):
        K, N = w2d.shape
        return np.ascontiguousarray(
            w2d.reshape(nch, 128, N).transpose(1, 0, 2))

    def colvec(v, nch):
        return np.ascontiguousarray(v.reshape(nch, 128).T)

    sh = {}
    sh["wqkv"] = pm(np.asarray(inputs["L0_attn_Wqkv"], np.float32), NCH)
    sh["bqkv"] = colvec(np.asarray(inputs["L0_attn_bqkv"], np.float32), 24)
    sh["wout"] = pm(np.asarray(inputs["L0_attn_Wout"], np.float32), NCH)
    sh["bout"] = colvec(np.asarray(inputs["L0_attn_bout"], np.float32), NCH)
    sh["wgate"] = pm(np.asarray(inputs["L0_attn_Wgate"], np.float32), NCH)
    sh["bgate"] = colvec(np.asarray(inputs["L0_attn_bgate"], np.float32), NCH)
    sh["pb"] = np.ascontiguousarray(np.broadcast_to(
        np.asarray(inputs["L0_attn_pos_bias"], np.float32).reshape(1, NOFF * H),
        (128, NOFF * H)))
    sh["lng"] = colvec(np.asarray(inputs["L0_ffn_ln_g"], np.float32), NCH)
    sh["lnb"] = colvec(np.asarray(inputs["L0_ffn_ln_b"], np.float32), NCH)
    sh["aw1"] = pm(np.asarray(inputs["L0_ffn_aw1"], np.float32), NCH)
    sh["ab1"] = colvec(np.asarray(inputs["L0_ffn_ab1"], np.float32), 32)
    sh["aw2"] = pm(np.asarray(inputs["L0_ffn_aw2"], np.float32), 32)
    sh["ab2"] = colvec(np.asarray(inputs["L0_ffn_ab2"], np.float32), NCH)
    sh["gatew"] = pm(np.asarray(inputs["L0_ffn_gate_w"], np.float32), NCH)
    sh["gateb"] = np.ascontiguousarray(np.broadcast_to(
        np.asarray(inputs["L0_ffn_gate_b"], np.float32).reshape(1, E_EXP),
        (128, E_EXP)))
    ew1 = np.asarray(inputs["L0_ffn_ew1"], np.float32)
    sh["ew1"] = np.ascontiguousarray(
        ew1.reshape(E_EXP, NCH, 128, HEXP).transpose(0, 2, 1, 3))
    sh["eb1"] = colvec(np.asarray(inputs["L0_ffn_eb1"], np.float32)
                       .reshape(E_EXP * HEXP), E_EXP * NHC)
    ew2 = np.asarray(inputs["L0_ffn_ew2"], np.float32)
    sh["ew2"] = np.ascontiguousarray(
        ew2.reshape(E_EXP, NHC, 128, D).transpose(0, 2, 1, 3))
    sh["eb2"] = np.ascontiguousarray(np.asarray(inputs["L0_ffn_eb2"], np.float32))
    gens = np.asarray(inputs["L0_ffn_gens"], np.float32)
    sh["gens"] = np.ascontiguousarray(
        gens.reshape(4, NCH, 128, D).transpose(0, 2, 1, 3))
    sh["gcw"] = colvec(np.asarray(inputs["L0_ffn_gc_w"], np.float32)
                       .reshape(2 * D), 2 * NCH)
    sh["gcb"] = np.asarray(inputs["L0_ffn_gc_b"], np.float32).reshape(1, 1)

    maps = []
    for core in range(8):
        seq, half = core // 2, core % 2
        xo = pm(np.ascontiguousarray(
            x[seq, half * TQ:(half + 1) * TQ, :].T), NCH)
        xc = np.zeros((128, NCH, TCTX), np.float32)
        mk = np.zeros((128, TCTX), np.float32)
        if half == 1:
            xc[:, :, 1024:] = pm(np.ascontiguousarray(x[seq, 0:TQ, :].T), NCH)
            mk[:, 1024:] = 1.0
        m = {"x_own": xo, "x_ctx": xc, "maskc": mk}
        m.update(sh)
        maps.append(m)
    return maps


def kernel(**inputs):
    nc = build_program()
    maps = host_inputs(inputs)
    res = run_bass_kernel_spmd(nc, maps, core_ids=list(range(8)))
    out = np.zeros((4, 4096, D), dtype=np.float32)
    for core in range(8):
        seq, half = core // 2, core % 2
        o = np.asarray(res.results[core]["out"])
        out[seq, half * TQ:(half + 1) * TQ, :] = \
            o.transpose(1, 0, 2).reshape(D, TQ).T
    return out


# version 02
# speedup vs baseline: 1.0432x; 1.0432x over previous
import numpy as np
from contextlib import ExitStack

import concourse.bass as bass
import concourse.mybir as mybir
from concourse.bass_utils import run_bass_kernel_spmd

F32 = mybir.dt.float32
F16 = mybir.dt.float16
ALU = mybir.AluOpType
ACT = mybir.ActivationFunctionType
AXX = mybir.AxisListType.X

D = 1024
NCH = 8
H = 16
HD = 64
TQ = 2048
TCTX = 3072
TKB = TCTX + TQ
TT = 512
NT_OWN = TQ // TT
NT_CTX = TCTX // TT
NT_ALL = TKB // TT
OFFS = [0, 1, 2, 3, 4, 6, 8, 12, 16, 24, 32, 48, 64, 96, 128, 192, 256, 384,
        512, 768, 1024, 1536, 2048, 3072]
NOFF = 24
NEAR_MAX = 192
LN_EPS = 1e-5
E_EXP = 6
HEXP = 2048
NHC = HEXP // 128
A4 = 4096

EXPM_A = [0.9999999999999889, 0.9999999999983087, 0.49999999999954886,
          0.1666666666460249, 0.04166666666357168, 0.008333333260441621,
          0.0013888888807935605, 0.00019841258487860976,
          2.4801576895938554e-05, 2.7556410718316194e-06,
          2.7556592217755425e-07, 2.5013109003262733e-08,
          2.0848571410855793e-09, 1.5202830621755456e-10,
          1.0899443294993002e-11]

ENGINES = ("sync", "tensor", "vector", "scalar", "gpsimd")

DEBUG = False
PHASE_LIMIT = 99


class Prog:
    def __init__(self, nc, stack):
        self.nc = nc
        self.steps = {e: [] for e in ENGINES}
        self.sem = {}
        self.cnt = {}
        self.waited = {e: {} for e in ENGINES}
        for name in ("ld", "gd", "mm", "vv", "aa", "gg"):
            self.sem[name] = stack.enter_context(nc.semaphore(name))
            self.cnt[name] = 0

    def op(self, eng, fn, waits=(), inc=None, inc_by=1):
        real = []
        for t in waits:
            if t is None:
                continue
            s, v = t
            if v > 0 and self.waited[eng].get(s, 0) < v:
                self.waited[eng][s] = v
                real.append((s, v))
        tok = None
        if inc is not None:
            self.cnt[inc] += inc_by
            tok = (inc, self.cnt[inc])
        sems = self.sem

        def closure(e, fn=fn, real=tuple(real), inc=inc, inc_by=inc_by):
            for s, v in real:
                e.wait_ge(sems[s], v)
            ins = fn(e)
            if inc is not None:
                ins.then_inc(sems[inc], inc_by)
        self.steps[eng].append(closure)
        return tok

    def wait(self, eng, waits):
        real = []
        for t in waits:
            if t is None:
                continue
            s, v = t
            if v > 0 and self.waited[eng].get(s, 0) < v:
                self.waited[eng][s] = v
                real.append((s, v))
        if not real:
            return
        sems = self.sem

        def closure(e, real=tuple(real)):
            for s, v in real:
                e.wait_ge(sems[s], v)
        self.steps[eng].append(closure)

    def barrier(self):
        toks = [(s, self.cnt[s]) for s in self.cnt]
        for e in ENGINES:
            self.wait(e, toks)

    def dma(self, eng, out, in_, waits=()):
        sem = "ld" if eng == "sync" else "gd"
        return self.op(eng, lambda e, o=out, i=in_: e.dma_start(out=o, in_=i),
                       waits=waits, inc=sem, inc_by=16)


def build_program():
    nc = bass.Bass("TRN2", target_bir_lowering=False)
    stack = ExitStack()

    def dram_in(name, shape):
        return nc.dram_tensor(name, shape, F32, kind="ExternalInput").ap()

    def dram_scr(name, shape, dbg=False):
        kind = "ExternalOutput" if (DEBUG and dbg) else "Internal"
        return nc.dram_tensor(name, shape, F32, kind=kind).ap()

    x_own = dram_in("x_own", [128, NCH, TQ])
    x_ctx = dram_in("x_ctx", [128, NCH, TCTX])
    maskc = dram_in("maskc", [128, TCTX])
    wqkv = dram_in("wqkv", [128, NCH, 3 * D])
    bqkv = dram_in("bqkv", [128, 24])
    wout = dram_in("wout", [128, NCH, D])
    bout = dram_in("bout", [128, NCH])
    wgate = dram_in("wgate", [128, NCH, D])
    bgate = dram_in("bgate", [128, NCH])
    pb_in = dram_in("pb", [128, H * NOFF])
    lng = dram_in("lng", [128, NCH])
    lnb = dram_in("lnb", [128, NCH])
    aw1 = dram_in("aw1", [128, NCH, A4])
    ab1 = dram_in("ab1", [128, 32])
    aw2 = dram_in("aw2", [128, 32, D])
    ab2 = dram_in("ab2", [128, NCH])
    gatew = dram_in("gatew", [128, NCH, E_EXP])
    gateb = dram_in("gateb", [128, E_EXP])
    ew1 = dram_in("ew1", [E_EXP, 128, NCH, HEXP])
    eb1 = dram_in("eb1", [128, E_EXP * NHC])
    ew2 = dram_in("ew2", [E_EXP, 128, NHC, D])
    eb2 = dram_in("eb2", [E_EXP, D])
    gens = dram_in("gens", [4, 128, NCH, D])
    gcw = dram_in("gcw", [128, 2 * NCH])
    gcb = dram_in("gcb", [1, 1])

    out_t = nc.dram_tensor("out", [128, NCH, TQ], F32, kind="ExternalOutput").ap()

    xn_sp = dram_scr("xn_sp", [128, NCH, TKB], dbg=True)
    q_sp = dram_scr("q_sp", [128, NCH, TQ], dbg=True)
    k_sp = dram_scr("k_sp", [128, NCH, TKB], dbg=True)
    v_sp = dram_scr("v_sp", [128, NCH, TKB], dbg=True)
    gate_sp = dram_scr("gate_sp", [128, NCH, TQ], dbg=True)
    attn_sp = dram_scr("attn_sp", [128, NCH, TQ], dbg=True)
    xn2_sp = dram_scr("xn2_sp", [128, NCH, TQ], dbg=True)
    abel_sp = dram_scr("abel_sp", [128, NCH, TQ], dbg=True)
    macc_sp = dram_scr("macc_sp", [128, NCH, TQ], dbg=True)
    expm_sp = dram_scr("expm_sp", [4, 128, NCH, D], dbg=True)
    nona_sp = dram_scr("nona_sp", [128, NCH, TQ], dbg=True)
    wtok_dbg = dram_scr("wtok_dbg", [128, 16 * E_EXP], dbg=True)
    WOFF = {"wqkv": 0, "wgate": 24576, "wout": 32768, "aw1": 40960,
            "aw2": 73728}
    for _e in range(E_EXP):
        WOFF[f"ew1_{_e}"] = 106496 + _e * 16384
        WOFF[f"ew2_{_e}"] = 204800 + _e * 16384
    WTOT = 303104
    wh_sp = nc.dram_tensor("wh_sp", [128, WTOT], F16, kind="Internal").ap()
    wl_sp = nc.dram_tensor("wl_sp", [128, WTOT], F16, kind="Internal").ap()
    probs_dbg = dram_scr("probs_dbg", [128, NT_OWN * 4, NOFF, H], dbg=True)

    sbuf = lambda name, shape: stack.enter_context(nc.sbuf_tensor(name, shape, F32))

    AR_W = sbuf("AR_W", [128, 32768]).ap()
    AR_X = sbuf("AR_X", [128, 16384]).ap()
    AW16 = AR_W.bitcast(F16)
    AX16 = AR_X.bitcast(F16)
    S = {}
    for name, shape in [
        ("ones_col", [128, 1]), ("ones_row", [1, 128]), ("iden", [128, 128]),
        ("iota_c", [128, 1]), ("iota_r16", [128, 16]), ("hrow", [128, 2]),
        ("sel", [128, NCH * 16]), ("bc", [16, NCH * 128]), ("jdiv", [16, 128]),
        ("bqkv", [128, 24]), ("bq8", [128, 8]), ("bout", [128, NCH]),
        ("bgate", [128, NCH]), ("pb", [128, H * NOFF]), ("lng", [128, NCH]),
        ("lnb", [128, NCH]), ("ab1", [128, 32]), ("ab2", [128, NCH]),
        ("gateb", [128, E_EXP]), ("eb1", [128, E_EXP * NHC]),
        ("eb2", [E_EXP, D]), ("gcw", [128, 2 * NCH]), ("gcb", [1, 1]),
        ("st16", [128, 64]),
        ("wtok", [128, 16 * E_EXP]), ("wfm", [E_EXP, TT]),
        ("mrow", [128, 4 * H]), ("tmp6", [128, 2 * E_EXP]),
    ]:
        S[name] = sbuf("s_" + name, shape).ap()

    ps = [stack.enter_context(nc.psum_tensor(f"psb{i}", [128, 512], F32)) for i in range(8)]

    P = Prog(nc, stack)
    bank_free = [None] * 8

    def acquire(b):
        t = bank_free[b]
        return ps[b], ([t] if t else [])

    def release(b, tok):
        bank_free[b] = tok

    t_io = P.op("gpsimd", lambda e: e.iota(S["iota_c"], pattern=[[0, 1]],
                channel_multiplier=1, allow_small_or_imprecise_dtypes=True),
                inc="gg")
    t_io2 = P.op("gpsimd", lambda e: e.iota(S["iota_r16"], pattern=[[1, 16]],
                 channel_multiplier=0, allow_small_or_imprecise_dtypes=True),
                 inc="gg")
    t_io3 = P.op("gpsimd", lambda e: e.iota(
        S["jdiv"].rearrange("p (a b) -> p a b", a=2), pattern=[[1, 2], [0, 64]],
        channel_multiplier=0, allow_small_or_imprecise_dtypes=True), inc="gg")
    t_ir = P.op("gpsimd", lambda e: e.iota(S["iden"], pattern=[[1, 128]],
                channel_multiplier=0, allow_small_or_imprecise_dtypes=True),
                inc="gg")
    t_idn = P.op("vector", lambda e: e.tensor_scalar(
        S["iden"], S["iden"], S["iota_c"][:, 0:1], None, ALU.is_equal),
        waits=[t_ir, t_io], inc="vv")
    t_on1 = P.op("vector", lambda e: e.memset(S["ones_col"], 1.0), inc="vv")
    t_on2 = P.op("vector", lambda e: e.memset(S["ones_row"], 1.0), inc="vv")
    t_hr = P.op("vector", lambda e: e.tensor_scalar(
        S["hrow"][:, 0:1], S["iota_c"], 64.0, None, ALU.is_ge),
        waits=[t_io], inc="vv")
    sel_last = None
    for c in range(NCH):
        t1 = P.op("vector", lambda e, c=c: e.tensor_scalar(
            S["hrow"][:, 1:2], S["hrow"][:, 0:1], float(2 * c), None, ALU.add),
            waits=[t_hr, sel_last], inc="vv")
        P.wait("vector", [t1])
        sel_last = P.op("vector", lambda e, c=c: e.tensor_scalar(
            S["sel"][:, c * 16:(c + 1) * 16], S["iota_r16"],
            S["hrow"][:, 1:2], None, ALU.is_equal),
            waits=[t_io2], inc="vv")
    bc_last = None
    for c in range(NCH):
        t1 = P.op("vector", lambda e, c=c: e.tensor_scalar(
            S["bc"][:, c * 128:(c + 1) * 128], S["jdiv"], float(2 * c), None,
            ALU.add), waits=[t_io3], inc="vv")
        P.wait("vector", [t1])
        bc_last = P.op("vector", lambda e, c=c: e.tensor_scalar(
            S["bc"][:, c * 128:(c + 1) * 128], S["bc"][:, c * 128:(c + 1) * 128],
            S["iota_c"][0:16, 0:1], None, ALU.is_equal),
            waits=[t_io], inc="vv")
    for name, src in [("bqkv", bqkv), ("bout", bout), ("bgate", bgate),
                      ("pb", pb_in), ("lng", lng), ("lnb", lnb), ("ab1", ab1),
                      ("ab2", ab2), ("gateb", gateb), ("eb1", eb1),
                      ("eb2", eb2), ("gcw", gcw), ("gcb", gcb)]:
        P.dma("sync", S[name], src)
    ld_all = ("ld", P.cnt["ld"])
    t_bq8 = P.op("vector", lambda e: e.tensor_scalar(
        S["bq8"], S["bqkv"][:, 0:8], 0.125, None, ALU.mult),
        waits=[ld_all], inc="vv")
    P.barrier()

    split_jobs = [("wqkv", wqkv, 24576), ("wgate", wgate, 8192),
                  ("wout", wout, 8192), ("aw1", aw1, 32768),
                  ("aw2", aw2, 32768)]
    for _e in range(E_EXP):
        split_jobs.append((f"ew1_{_e}", ew1[_e], 16384))
        split_jobs.append((f"ew2_{_e}", ew2[_e], 16384))
    sp_src = [AR_X[:, 0:4096], AR_X[:, 4096:8192]]
    sp_hi = [AX16[:, 16384:20480], AX16[:, 20480:24576]]
    sp_lo = [AX16[:, 24576:28672], AX16[:, 28672:32768]]
    sp_prev = {}
    sp_i = 0
    for name, ap_, elems in split_jobs:
        flat = ap_.rearrange("p c n -> p (c n)")
        off = WOFF[name]
        for t0 in range(0, elems, 4096):
            sl = sp_i % 2
            t_ld = P.dma("sync", sp_src[sl], flat[:, t0:t0 + 4096],
                         waits=[sp_prev.get(("lo", sl))])
            th = P.op("vector", lambda e, sl=sl: e.tensor_copy(
                sp_hi[sl], sp_src[sl]),
                waits=[t_ld, sp_prev.get(("st", sl))], inc="vv")
            P.wait("vector", [th])
            tl = P.op("vector", lambda e, sl=sl: e.tensor_sub(
                sp_lo[sl], sp_src[sl], sp_hi[sl]), inc="vv")
            sp_prev[("lo", sl)] = tl
            t_s1 = P.dma("gpsimd", wh_sp[:, off + t0:off + t0 + 4096],
                         sp_hi[sl], waits=[tl])
            t_s2 = P.dma("gpsimd", wl_sp[:, off + t0:off + t0 + 4096],
                         sp_lo[sl], waits=[tl])
            sp_prev[("st", sl)] = t_s2
            sp_i += 1
    P.barrier()

    def emit_ln(src_fn, ntiles, dst_sp, gvec=None, bvec=None):
        xa = AR_X[:, 0:4096].rearrange("p (c t) -> p c t", c=NCH)
        sq = AR_X[:, 4096:8192].rearrange("p (c t) -> p c t", c=NCH)
        xn = AR_X[:, 8192:12288].rearrange("p (c t) -> p c t", c=NCH)
        rows = AR_X[0:1, 12288:14848]
        mu, m2 = rows[:, 0:TT], rows[:, TT:2 * TT]
        va, rs = rows[:, 2 * TT:3 * TT], rows[:, 3 * TT:4 * TT]
        na = rows[:, 4 * TT:5 * TT]
        prev = {}
        for j in range(ntiles):
            t_ld = P.dma("sync", xa, src_fn(j),
                         waits=[prev.get("xa_done"), prev.get("st")])
            t_sq = P.op("scalar", lambda e: e.activation(
                sq[:, :, :], xa[:, :, :], ACT.Square),
                waits=[t_ld, prev.get("sq_done")], inc="aa")
            b6, w6 = acquire(6)
            b7, w7 = acquire(7)
            tmm = None
            for c in range(NCH):
                P.op("tensor", lambda e, c=c, b=b6: e.matmul(
                    b[0:1, :], S["ones_col"], xa[:, c, :],
                    start=(c == 0), stop=(c == NCH - 1)),
                    waits=([t_ld, t_on1] + w6) if c == 0 else [])
            for c in range(NCH):
                tmm = P.op("tensor", lambda e, c=c, b=b7: e.matmul(
                    b[0:1, :], S["ones_col"], sq[:, c, :],
                    start=(c == 0), stop=(c == NCH - 1)),
                    waits=([t_sq] + w7) if c == 0 else [],
                    inc="mm" if c == NCH - 1 else None)
            t1 = P.op("vector", lambda e, b=b6: e.tensor_scalar(
                mu, b[0:1, :], 1.0 / D, None, ALU.mult),
                waits=[tmm, prev.get("rows_done")], inc="vv")
            t2 = P.op("vector", lambda e, b=b7: e.tensor_scalar(
                m2, b[0:1, :], 1.0 / D, None, ALU.mult), inc="vv")
            P.wait("vector", [t1, t2])
            t3 = P.op("vector", lambda e: e.tensor_mul(va, mu, mu), inc="vv")
            P.wait("vector", [t3])
            t4 = P.op("vector", lambda e: e.tensor_sub(va, m2, va), inc="vv")
            P.wait("vector", [t4])
            t5 = P.op("vector", lambda e: e.tensor_scalar(
                va, va, LN_EPS, None, ALU.add), inc="vv")
            t6 = P.op("scalar", lambda e: e.activation(na, va, ACT.Sqrt),
                      waits=[t5], inc="aa")
            t7 = P.op("vector", lambda e: e.reciprocal(rs, na), waits=[t6],
                      inc="vv")
            release(6, t7)
            release(7, t7)
            b4, w4 = acquire(4)
            b5, w5 = acquire(5)
            t8 = P.op("tensor", lambda e, b=b4: e.matmul(
                b[:, :], S["ones_row"], mu, start=True, stop=True),
                waits=[t_on2, t1] + w4, inc="mm")
            t9 = P.op("tensor", lambda e, b=b5: e.matmul(
                b[:, :], S["ones_row"], rs, start=True, stop=True),
                waits=[t7] + w5, inc="mm")
            toks = []
            for c in range(NCH):
                ta = P.op("vector", lambda e, c=c, b=b4: e.tensor_sub(
                    xn[:, c, :], xa[:, c, :], b[:, :]),
                    waits=[t8, t9, prev.get("st")], inc="vv")
                P.wait("vector", [ta])
                tb = P.op("vector", lambda e, c=c, b=b5: e.tensor_mul(
                    xn[:, c, :], xn[:, c, :], b[:, :]), inc="vv")
                if gvec is not None:
                    P.wait("vector", [tb])
                    tb = P.op("vector", lambda e, c=c: e.tensor_scalar(
                        xn[:, c, :], xn[:, c, :], gvec[:, c:c + 1],
                        bvec[:, c:c + 1], ALU.mult, ALU.add), inc="vv")
                toks.append(tb)
            release(4, toks[-1])
            release(5, toks[-1])
            t_st = P.dma("gpsimd", dst_sp[:, :, j * TT:(j + 1) * TT],
                         xn[:, :, :], waits=toks)
            prev = {"xa_done": toks[-1], "st": t_st, "sq_done": tmm,
                    "rows_done": t9}
        P.barrier()

    def x_src(j):
        if j < NT_CTX:
            return x_ctx[:, :, j * TT:(j + 1) * TT]
        jo = j - NT_CTX
        return x_own[:, :, jo * TT:(jo + 1) * TT]

    emit_ln(x_src, NT_ALL, xn_sp)

    if PHASE_LIMIT >= 2:
        wqh = AW16[:, 0:24576].rearrange("p (c n) -> p c n", c=NCH)
        wql = AW16[:, 24576:49152].rearrange("p (c n) -> p c n", c=NCH)
        xb = AR_X[:, 0:4096].rearrange("p (c t) -> p c t", c=NCH)
        xbf = AR_X[:, 0:4096]
        xphf = AX16[:, 24576:28672]
        xplf = AX16[:, 28672:32768]
        xph = xphf.rearrange("p (c t) -> p c t", c=NCH)
        xpl = xplf.rearrange("p (c t) -> p c t", c=NCH)
        ev = AR_X[:, 4096:4608]
        mkv = AR_X[:, 4608:7680]
        t_wh = P.dma("sync", AW16[:, 0:24576], wh_sp[:, 0:24576])
        t_wl = P.dma("sync", AW16[:, 24576:49152], wl_sp[:, 0:24576])
        t_mk = P.dma("sync", mkv, maskc)
        prev_ld = None
        prev_sp = None
        for j in range(NT_ALL):
            own = j >= NT_CTX
            t_ld = P.dma("sync", xb, xn_sp[:, :, j * TT:(j + 1) * TT],
                         waits=[prev_ld])
            th = P.op("vector", lambda e: e.tensor_copy(xphf, xbf),
                      waits=[t_ld, prev_sp], inc="vv")
            P.wait("vector", [th])
            tsp = P.op("vector", lambda e: e.tensor_sub(xplf, xbf, xphf),
                       inc="vv")
            dcs = list(range(24)) if own else list(range(8, 24))
            last_st = None
            tmm = None
            for dc in dcs:
                bnk = dc % 8
                b, wb = acquire(bnk)
                first = True
                for xs, ws in ((xph, wqh), (xpl, wqh), (xph, wql)):
                    for c in range(NCH):
                        last3 = (ws is wql) and (c == NCH - 1)
                        tmm = P.op("tensor",
                            lambda e, c=c, dc=dc, b=b, xs=xs, ws=ws,
                            first=(first and c == 0), last3=last3: e.matmul(
                                b[:, :], ws[:, c, dc * 128:(dc + 1) * 128],
                                xs[:, c, :], start=first, stop=last3),
                            waits=([t_wh, t_wl, tsp] + wb)
                            if (first and c == 0) else [],
                            inc="mm" if last3 else None)
                    first = False
                is_q = dc < 8
                bias = S["bq8"][:, dc:dc + 1] if is_q else S["bqkv"][:, dc:dc + 1]
                scale = 0.125 if is_q else 1.0
                tev = P.op("scalar", lambda e, b=b, bias=bias, scale=scale:
                           e.activation(ev, b[:, :], ACT.Identity, bias=bias,
                                        scale=scale),
                           waits=[tmm, t_bq8, last_st], inc="aa")
                if (not own) and (not is_q):
                    tev = P.op("vector", lambda e, j=j: e.tensor_mul(
                        ev, ev, mkv[:, j * TT:(j + 1) * TT]),
                        waits=[tev, t_mk], inc="vv")
                release(bnk, tev)
                if is_q:
                    jo = j - NT_CTX
                    dst = q_sp[:, dc, jo * TT:(jo + 1) * TT]
                elif dc < 16:
                    dst = k_sp[:, dc - 8, j * TT:(j + 1) * TT]
                else:
                    dst = v_sp[:, dc - 16, j * TT:(j + 1) * TT]
                last_st = P.dma("gpsimd", dst, ev, waits=[tev])
            prev_ld = th
            prev_ld = tsp
            prev_sp = tmm
        P.barrier()

        wgh = AW16[:, 0:8192].rearrange("p (c n) -> p c n", c=NCH)
        wgl = AW16[:, 8192:16384].rearrange("p (c n) -> p c n", c=NCH)
        t_wh = P.dma("sync", AW16[:, 0:8192],
                     wh_sp[:, WOFF["wgate"]:WOFF["wgate"] + 8192])
        t_wl = P.dma("sync", AW16[:, 8192:16384],
                     wl_sp[:, WOFF["wgate"]:WOFF["wgate"] + 8192])
        prev_ld = None
        prev_sp = None
        for jo in range(NT_OWN):
            j = NT_CTX + jo
            t_ld = P.dma("sync", xb, xn_sp[:, :, j * TT:(j + 1) * TT],
                         waits=[prev_ld])
            th = P.op("vector", lambda e: e.tensor_copy(xphf, xbf),
                      waits=[t_ld, prev_sp], inc="vv")
            P.wait("vector", [th])
            tsp = P.op("vector", lambda e: e.tensor_sub(xplf, xbf, xphf),
                       inc="vv")
            last_st = None
            tmm = None
            for dc in range(8):
                b, wb = acquire(dc)
                first = True
                for xs, ws in ((xph, wgh), (xpl, wgh), (xph, wgl)):
                    for c in range(NCH):
                        last3 = (ws is wgl) and (c == NCH - 1)
                        tmm = P.op("tensor",
                            lambda e, c=c, dc=dc, b=b, xs=xs, ws=ws,
                            first=(first and c == 0), last3=last3: e.matmul(
                                b[:, :], ws[:, c, dc * 128:(dc + 1) * 128],
                                xs[:, c, :], start=first, stop=last3),
                            waits=([t_wh, t_wl, tsp] + wb)
                            if (first and c == 0) else [],
                            inc="mm" if last3 else None)
                    first = False
                tev = P.op("scalar", lambda e, b=b, dc=dc: e.activation(
                    ev, b[:, :], ACT.Sigmoid, bias=S["bgate"][:, dc:dc + 1],
                    scale=1.0), waits=[tmm, last_st], inc="aa")
                release(dc, tev)
                last_st = P.dma("gpsimd", gate_sp[:, dc, jo * TT:(jo + 1) * TT],
                                ev, waits=[tev])
            prev_ld = tsp
            prev_sp = tmm
        P.barrier()

    if PHASE_LIMIT >= 3:
        qt = AR_W[:, 0:4096].rearrange("p (c t) -> p c t", c=NCH)
        kwin = AR_W[:, 4096:9728].rearrange("p (c t) -> p c t", c=NCH)
        kfar = [AR_W[:, 9728 + i * 4096:9728 + (i + 1) * 4096]
                .rearrange("p (c t) -> p c t", c=NCH) for i in range(2)]
        pbuf = [AR_W[:, 17920 + i * 4096:17920 + (i + 1) * 4096]
                .rearrange("p (c t) -> p c t", c=NCH) for i in range(2)]
        accb = AR_W[:, 26112:30208].rearrange("p (c t) -> p c t", c=NCH)
        gt_t = AR_X[:, 0:4096].rearrange("p (c t) -> p c t", c=NCH)
        att = AR_X[:, 4096:8192].rearrange("p (c t) -> p c t", c=NCH)
        sm1 = AR_X[:, 4096:5632].rearrange("p (s o h) -> p s o h", s=4, o=NOFF)
        tmpv2 = [AR_X[:, 8192:8704], AR_X[:, 11776:12288]]
        pfm_a = AR_X[:, 8704:9728]
        probs = AR_X[:, 9728:11264].rearrange("p (s o h) -> p s o h", s=4, o=NOFF)
        rdrow_buf = AR_X[0:16, 11264:11776]
        mrow = S["mrow"].rearrange("p (s h) -> p s h", s=4)
        pb_v = S["pb"].rearrange("p (o h) -> p o h", o=NOFF)
        far_offs = [(o, d) for o, d in enumerate(OFFS) if d > NEAR_MAX]

        for jo in range(NT_OWN):
            t0 = jo * TT
            kq0 = TCTX + t0
            t_q = P.dma("sync", qt, q_sp[:, :, t0:t0 + TT])
            t_nw = P.dma("sync", kwin, k_sp[:, :, kq0 - NEAR_MAX:kq0 + TT])
            sbank = []
            for s in range(4):
                b, wb = acquire(s)
                sbank.append(b)
                P.wait("tensor", wb)
            prod_toks = {}
            far_ld = {}
            sel_last = None
            for o, dlt in enumerate(OFFS):
                pslot = pbuf[o % 2]
                if dlt <= NEAR_MAX:
                    ksrc = kwin[:, :, NEAR_MAX - dlt:NEAR_MAX - dlt + TT]
                    kw = t_nw
                else:
                    fi = far_offs.index((o, dlt))
                    slot = kfar[fi % 2]
                    freew = prod_toks.get(far_offs[fi - 2][0]) if fi >= 2 else None
                    far_ld[fi] = P.dma("sync", slot,
                                       k_sp[:, :, kq0 - dlt:kq0 - dlt + TT],
                                       waits=[freew])
                    ksrc = slot
                    kw = far_ld[fi]
                tp = None
                for c in range(NCH):
                    tp = P.op("vector", lambda e, c=c, ksrc=ksrc, pslot=pslot:
                              e.tensor_mul(pslot[:, c, :], qt[:, c, :],
                                           ksrc[:, c, :]),
                              waits=[t_q, kw, prod_toks.get(o - 2)]
                              if c == 0 else [], inc="vv" if c == NCH - 1
                              else None)
                for s in range(4):
                    for c in range(NCH):
                        sel_last = P.op("tensor",
                            lambda e, s=s, c=c, o=o, pslot=pslot: e.matmul(
                                sbank[s][:, o * 16:(o + 1) * 16],
                                pslot[:, c, s * 128:(s + 1) * 128],
                                S["sel"][:, c * 16:(c + 1) * 16],
                                start=(c == 0), stop=(c == NCH - 1)),
                            waits=[tp] if c == 0 else [],
                            inc="mm" if c == NCH - 1 else None)
                prod_toks[o] = sel_last
            P.barrier()
            sf_done = []
            rd_done = []
            for s in range(4):
                t1 = P.op("vector", lambda e, s=s: e.tensor_add(
                    sm1[:, s, :, :],
                    sbank[s][:, 0:NOFF * 16].rearrange("p (o h) -> p o h",
                                                       o=NOFF), pb_v),
                    waits=[prod_toks[NOFF - 1]], inc="vv")
                t4 = P.op("scalar", lambda e, s=s: e.activation(
                    probs[:, s, :, :], sm1[:, s, :, :], ACT.Exp),
                    waits=[t1], inc="aa")
                t5 = P.op("vector", lambda e, s=s: e.reduce_sum(
                    mrow[:, s, :],
                    probs[:, s, :, :].rearrange("p o h -> p h o"), axis=AXX),
                    waits=[t4], inc="vv")
                P.wait("vector", [t5])
                t6 = P.op("vector", lambda e, s=s: e.reciprocal(
                    S["st16"][:, s * 16:(s + 1) * 16], mrow[:, s, :]),
                    inc="vv")
                sf_done.append(t4)
                rd_done.append(t6)
            b7r, w7r = acquire(7)
            ttr = None
            for s in range(4):
                ttr = P.op("tensor", lambda e, s=s, b=b7r: e.transpose(
                    b[0:16, s * 128:(s + 1) * 128],
                    S["st16"][:, s * 16:(s + 1) * 16], S["iden"]),
                    waits=([rd_done[-1], t_idn] + w7r) if s == 0 else [],
                    inc="mm" if s == 3 else None)
            t_rd = P.op("vector", lambda e, b=b7r: e.tensor_copy(
                rdrow_buf[0:16, :], b[0:16, 0:TT]), waits=[ttr], inc="vv")
            release(7, t_rd)
            if DEBUG:
                P.dma("gpsimd", probs_dbg[:, jo * 4:(jo + 1) * 4, :, :],
                      probs[:, :, :, :], waits=sf_done)
            P.barrier()
            t_ac = P.op("vector", lambda e: e.memset(accb[:, :, :], 0.0),
                        waits=[sf_done[-1]], inc="vv")
            t_vw = P.dma("sync", kwin, v_sp[:, :, kq0 - NEAR_MAX:kq0 + TT],
                         waits=[prod_toks[NOFF - 1]])
            mac_toks = {}
            farv_ld = {}
            for o, dlt in enumerate(OFFS):
                pslot = pfm_a[0:16, (o % 2) * TT:(o % 2) * TT + TT]
                b6, w6 = acquire(6)
                ttr = None
                for s in range(4):
                    ttr = P.op("tensor", lambda e, s=s, o=o, b=b6: e.transpose(
                        b[0:16, s * 128:(s + 1) * 128], probs[:, s, o, :],
                        S["iden"]),
                        waits=([sf_done[-1], t_idn] + w6) if s == 0 else [],
                        inc="mm" if s == 3 else None)
                tcp = P.op("vector", lambda e, b=b6, pslot=pslot: e.tensor_copy(
                    pslot[0:16, :], b[0:16, 0:TT]),
                    waits=[ttr, mac_toks.get(o - 2)], inc="vv")
                release(6, tcp)
                if dlt <= NEAR_MAX:
                    vsrc = kwin[:, :, NEAR_MAX - dlt:NEAR_MAX - dlt + TT]
                    vw = t_vw
                else:
                    fi = far_offs.index((o, dlt))
                    freew = mac_toks.get(far_offs[fi - 2][0]) if fi >= 2 else None
                    farv_ld[fi] = P.dma("sync", kfar[fi % 2],
                                        v_sp[:, :, kq0 - dlt:kq0 - dlt + TT],
                                        waits=[freew, far_ld.get(fi)])
                    vsrc = kfar[fi % 2]
                    vw = farv_ld[fi]
                last = None
                for c in range(NCH):
                    bnk = 4 + (c % 2)
                    tv = tmpv2[c % 2]
                    b, wb = acquire(bnk)
                    tb = P.op("tensor", lambda e, c=c, b=b, pslot=pslot:
                              e.matmul(b[:, :], S["bc"][:, c * 128:(c + 1) * 128],
                                       pslot[0:16, :], start=True, stop=True),
                              waits=[tcp] + wb, inc="mm")
                    tm = P.op("vector", lambda e, c=c, b=b, vsrc=vsrc, tv=tv:
                              e.tensor_mul(tv, vsrc[:, c, :], b[:, :]),
                              waits=[tb, vw, t_ac], inc="vv")
                    P.wait("vector", [tm])
                    last = P.op("vector", lambda e, c=c, tv=tv: e.tensor_add(
                        accb[:, c, :], accb[:, c, :], tv), inc="vv")
                    release(bnk, last)
                mac_toks[o] = last
            P.barrier()
            t_g = P.dma("sync", gt_t, gate_sp[:, :, t0:t0 + TT])
            gm = None
            for c in range(NCH):
                bnk = 4 + (c % 2)
                b, wb = acquire(bnk)
                tb = P.op("tensor", lambda e, c=c, b=b: e.matmul(
                    b[:, :], S["bc"][:, c * 128:(c + 1) * 128],
                    rdrow_buf[0:16, :], start=True, stop=True),
                    waits=[t_rd] + wb, inc="mm")
                tv = tmpv2[c % 2]
                tm = P.op("vector", lambda e, c=c, b=b, tv=tv: e.tensor_mul(
                    tv, gt_t[:, c, :], b[:, :]),
                    waits=[tb, t_g, mac_toks[NOFF - 1]], inc="vv")
                P.wait("vector", [tm])
                gm = P.op("vector", lambda e, c=c, tv=tv: e.tensor_mul(
                    accb[:, c, :], accb[:, c, :], tv), inc="vv")
                release(bnk, gm)
            wo = AR_W[:, 4096:12288].rearrange("p (c n) -> p c n", c=NCH)
            t_wo = P.dma("sync", wo, wout,
                         waits=[prod_toks[NOFF - 1], mac_toks[NOFF - 1]])
            pr_done = []
            for dc in range(8):
                b, wb = acquire(dc)
                tmm = None
                for c in range(NCH):
                    tmm = P.op("tensor", lambda e, c=c, dc=dc, b=b: e.matmul(
                        b[:, :], wo[:, c, dc * 128:(dc + 1) * 128],
                        accb[:, c, :], start=(c == 0), stop=(c == NCH - 1)),
                        waits=([t_wo, gm] + wb) if c == 0 else [],
                        inc="mm" if c == NCH - 1 else None)
                tev = P.op("scalar", lambda e, b=b, dc=dc: e.activation(
                    att[:, dc, :], b[:, :], ACT.Identity,
                    bias=S["bout"][:, dc:dc + 1], scale=1.0),
                    waits=[tmm], inc="aa")
                release(dc, tev)
                pr_done.append(tev)
            P.dma("gpsimd", attn_sp[:, :, t0:t0 + TT], att[:, :, :],
                  waits=pr_done)
            P.barrier()

    if PHASE_LIMIT >= 4:
        emit_ln(lambda j: xn_sp[:, :, TCTX + j * TT:TCTX + (j + 1) * TT],
                NT_OWN, xn2_sp, gvec=S["lng"], bvec=S["lnb"])

    if PHASE_LIMIT >= 5:
        w1s = [AR_W[:, i * 8192:(i + 1) * 8192]
               .rearrange("p (c n) -> p c n", c=NCH) for i in range(2)]
        w2s = [AR_W[:, 16384 + i * 8192:16384 + (i + 1) * 8192]
               .rearrange("p (c n) -> p c n", c=NCH) for i in range(2)]
        xb2 = AR_X[:, 0:4096].rearrange("p (c t) -> p c t", c=NCH)
        hg = AR_X[:, 4096:8192].rearrange("p (c t) -> p c t", c=NCH)
        abacc = AR_X[:, 8192:12288].rearrange("p (c t) -> p c t", c=NCH)
        prev_tile = {}
        for jo in range(NT_OWN):
            t_ld = P.dma("sync", xb2, xn2_sp[:, :, jo * TT:(jo + 1) * TT],
                         waits=[prev_tile.get("xb2")])
            w_toks = {}
            gel_done = {}
            prev_mm_done = {}
            acc_done = None
            for g in range(4):
                sl = g % 2
                w_toks[("w1", g)] = P.dma(
                    "sync", w1s[sl], aw1[:, :, g * 1024:(g + 1) * 1024],
                    waits=[gel_done.get(g - 2), prev_tile.get("w1")])
                w_toks[("w2", g)] = P.dma(
                    "sync", w2s[sl], aw2[:, g * 8:(g + 1) * 8, :],
                    waits=[acc_done if g >= 2 else None, prev_tile.get("w2")])
                tg = None
                for i in range(8):
                    dc = g * 8 + i
                    bnk = i % 4
                    b, wb = acquire(bnk)
                    tmm = None
                    for c in range(NCH):
                        tmm = P.op("tensor", lambda e, c=c, i=i, b=b, sl=sl:
                                   e.matmul(b[:, :],
                                            w1s[sl][:, c, i * 128:(i + 1) * 128],
                                            xb2[:, c, :], start=(c == 0),
                                            stop=(c == NCH - 1)),
                                   waits=([w_toks[("w1", g)], t_ld] + wb)
                                   if c == 0 else [],
                                   inc="mm" if c == NCH - 1 else None)
                    tg = P.op("scalar", lambda e, b=b, dc=dc, i=i: e.activation(
                        hg[:, i, :], b[:, :], ACT.Gelu,
                        bias=S["ab1"][:, dc:dc + 1], scale=1.0),
                        waits=[tmm, gel_done.get(g - 1),
                               prev_mm_done.get(g - 1)], inc="aa")
                    release(bnk, tg)
                gel_done[g] = tg
                for half in range(2):
                    for i in range(4):
                        dc = half * 4 + i
                        bnk = 4 + i
                        b, wb = acquire(bnk)
                        tmm = None
                        for c in range(NCH):
                            tmm = P.op("tensor",
                                lambda e, c=c, dc=dc, b=b, sl=sl: e.matmul(
                                    b[:, :],
                                    w2s[sl][:, c, dc * 128:(dc + 1) * 128],
                                    hg[:, c, :], start=(c == 0),
                                    stop=(c == NCH - 1)),
                                waits=([w_toks[("w2", g)], tg] + wb)
                                if c == 0 else [],
                                inc="mm" if c == NCH - 1 else None)
                        if g == 0:
                            tev = P.op("scalar", lambda e, b=b, dc=dc:
                                       e.activation(abacc[:, dc, :], b[:, :],
                                                    ACT.Identity,
                                                    bias=S["ab2"][:, dc:dc + 1],
                                                    scale=1.0),
                                       waits=[tmm], inc="aa")
                        else:
                            tev = P.op("vector", lambda e, b=b, dc=dc:
                                       e.tensor_add(abacc[:, dc, :],
                                                    abacc[:, dc, :], b[:, :]),
                                       waits=[tmm], inc="vv")
                        release(bnk, tev)
                        acc_done = tev
                prev_mm_done[g] = tmm
            t_st = P.dma("gpsimd", abel_sp[:, :, jo * TT:(jo + 1) * TT],
                         abacc[:, :, :], waits=[acc_done])
            prev_tile = {"xb2": tmm, "w1": tmm, "w2": tmm, "st": t_st}
            P.wait("sync", [t_st])
        P.barrier()

    if PHASE_LIMIT >= 6:
        gw = AR_W[:, 0:NCH * E_EXP].rearrange("p (c n) -> p c n", c=NCH)
        xb2 = AR_X[:, 0:4096].rearrange("p (c t) -> p c t", c=NCH)
        lt = S["tmp6"][:, 0:E_EXP]
        k6 = S["tmp6"][:, E_EXP:2 * E_EXP]
        m1 = S["st16"][:, 0:1]
        nm1 = S["st16"][:, 1:2]
        m2 = S["st16"][:, 2:3]
        den = S["st16"][:, 3:4]
        rin = S["st16"][:, 4:5]
        t_w = P.dma("sync", gw, gatew)
        prev_ld = None
        for jo in range(NT_OWN):
            t_ld = P.dma("sync", xb2, xn2_sp[:, :, jo * TT:(jo + 1) * TT],
                         waits=[prev_ld])
            for s in range(4):
                st = jo * 4 + s
                b, wb = acquire(s % 2)
                tmm = None
                for c in range(NCH):
                    tmm = P.op("tensor", lambda e, c=c, s=s, b=b: e.matmul(
                        b[:, 0:E_EXP], xb2[:, c, s * 128:(s + 1) * 128],
                        gw[:, c, :], start=(c == 0), stop=(c == NCH - 1)),
                        waits=([t_w, t_ld] + wb) if c == 0 else [],
                        inc="mm" if c == NCH - 1 else None)
                ta = P.op("vector", lambda e, b=b: e.tensor_add(
                    lt, b[:, 0:E_EXP], S["gateb"]), waits=[tmm], inc="vv")
                P.wait("vector", [ta])
                tb = P.op("vector", lambda e: e.reduce_max(m1, lt, axis=AXX),
                          inc="vv")
                P.wait("vector", [tb])
                tc = P.op("vector", lambda e: e.tensor_scalar(
                    nm1, m1, -1.0, None, ALU.mult), inc="vv")
                td = P.op("vector", lambda e: e.tensor_scalar(
                    k6, lt, m1[:, 0:1], None, ALU.is_ge), inc="vv")
                P.wait("vector", [tc, td])
                te = P.op("vector", lambda e: e.scalar_tensor_tensor(
                    k6, k6, -1e30, lt, ALU.mult, ALU.add), inc="vv")
                P.wait("vector", [te])
                tf = P.op("vector", lambda e: e.reduce_max(m2, k6, axis=AXX),
                          inc="vv")
                tex = P.op("scalar", lambda e: e.activation(
                    k6, lt, ACT.Exp, bias=nm1[:, 0:1], scale=1.0),
                    waits=[tc, tf], inc="aa")
                P.wait("vector", [tf, tex])
                tg2 = P.op("vector", lambda e: e.tensor_scalar(
                    lt, lt, m2[:, 0:1], None, ALU.is_ge), inc="vv")
                P.wait("vector", [tg2])
                th = P.op("vector", lambda e: e.tensor_mul(k6, k6, lt),
                          inc="vv")
                P.wait("vector", [th])
                ti = P.op("vector", lambda e: e.reduce_sum(den, k6, axis=AXX),
                          inc="vv")
                P.wait("vector", [ti])
                tj = P.op("vector", lambda e: e.reciprocal(rin, den), inc="vv")
                P.wait("vector", [tj])
                tk = P.op("vector", lambda e, st=st: e.tensor_scalar(
                    S["wtok"][:, st * E_EXP:(st + 1) * E_EXP], k6, rin[:, 0:1],
                    None, ALU.mult), inc="vv")
                release(s % 2, ta)
            prev_ld = tmm
        if DEBUG:
            P.dma("gpsimd", wtok_dbg, S["wtok"], waits=[("vv", P.cnt["vv"])])
        P.barrier()

    if PHASE_LIMIT >= 7:
        w1h = AW16[:, 0:16384].rearrange("p (c n) -> p c n", c=NCH)
        w1l = AW16[:, 16384:32768].rearrange("p (c n) -> p c n", c=NCH)
        w2h = AW16[:, 32768:49152].rearrange("p (c n) -> p c n", c=NHC)
        w2l = AW16[:, 49152:65536].rearrange("p (c n) -> p c n", c=NHC)
        xstg = AR_X[:, 4096:8192]
        xphf = AX16[:, 24576:28672]
        xplf = AX16[:, 28672:32768]
        xph = xphf.rearrange("p (c t) -> p c t", c=NCH)
        xpl = xplf.rearrange("p (c t) -> p c t", c=NCH)
        hhh = AX16[:, 0:4096].rearrange("p (c t) -> p c t", c=NCH)
        hhl = AX16[:, 4096:8192].rearrange("p (c t) -> p c t", c=NCH)
        hslot = [AR_X[:, 4096:4608], AR_X[:, 4608:5120]]
        mac = AR_X[:, 8192:12288].rearrange("p (c t) -> p c t", c=NCH)
        prev = {}
        for jo in range(NT_OWN):
            t_ld = P.dma("sync", xstg, xn2_sp[:, :, jo * TT:(jo + 1) * TT],
                         waits=[prev.get("xstg")])
            th = P.op("vector", lambda e: e.tensor_copy(xphf, xstg),
                      waits=[t_ld, prev.get("xpair")], inc="vv")
            P.wait("vector", [th])
            tsp = P.op("vector", lambda e: e.tensor_sub(xplf, xstg, xphf),
                       inc="vv")
            b7w, w7w = acquire(7)
            ttw0 = None
            for s_ in range(4):
                ttw0 = P.op("tensor", lambda e, s_=s_, jo=jo, b=b7w:
                            e.transpose(b[0:E_EXP, s_ * 128:(s_ + 1) * 128],
                                        S["wtok"][:, (jo * 4 + s_) * E_EXP:
                                                  (jo * 4 + s_ + 1) * E_EXP],
                                        S["iden"]),
                            waits=([t_idn, prev.get("wfm")] + w7w)
                            if s_ == 0 else [],
                            inc="mm" if s_ == 3 else None)
            t_wfm = P.op("vector", lambda e, b=b7w: e.tensor_copy(
                S["wfm"][0:E_EXP, :], b[0:E_EXP, 0:TT]), waits=[ttw0],
                inc="vv")
            release(7, t_wfm)
            eo_done = None
            h_done = None
            wprev1 = prev.get("w1")
            wprev2 = prev.get("w2")
            for ei in range(E_EXP):
                o1 = WOFF[f"ew1_{ei}"]
                o2 = WOFF[f"ew2_{ei}"]
                t_w1h = P.dma("sync", AW16[:, 0:16384],
                              wh_sp[:, o1:o1 + 16384], waits=[h_done, wprev1])
                t_w1l = P.dma("sync", AW16[:, 16384:32768],
                              wl_sp[:, o1:o1 + 16384])
                t_w2h = P.dma("sync", AW16[:, 32768:49152],
                              wh_sp[:, o2:o2 + 16384], waits=[eo_done, wprev2])
                t_w2l = P.dma("sync", AW16[:, 49152:65536],
                              wl_sp[:, o2:o2 + 16384])
                wprev1, wprev2 = t_w1l, t_w2l
                b6, w6 = acquire(6)
                wrow = AR_X[0:1, 5120:5632]
                ttw = None
                for s_ in range(4):
                    ttw = P.op("tensor", lambda e, ei=ei, jo=jo, s_=s_, b=b6:
                               e.transpose(b[0:1, s_ * 128:(s_ + 1) * 128],
                                           S["wtok"][:, (jo * 4 + s_) * E_EXP
                                                     + ei:(jo * 4 + s_) *
                                                     E_EXP + ei + 1],
                                           S["iden"]),
                               waits=([t_idn] + w6) if s_ == 0 else [],
                               inc="mm" if s_ == 3 else None)
                tcw = P.op("vector", lambda e, b=b6: e.tensor_copy(
                    wrow, b[0:1, 0:TT]), waits=[ttw, prev.get("wrow")],
                    inc="vv")
                twb = P.op("tensor", lambda e, b=b6: e.matmul(
                    b[:, :], S["ones_row"], wrow,
                    start=True, stop=True), waits=[t_on2, tcw], inc="mm")
                prev["wrow"] = twb
                last_eo_prev = eo_done
                for half in range(2):
                    h_done = None
                    for hc in range(NCH):
                        gc_ = half * NCH + hc
                        bnk = hc % 4
                        hs = hslot[hc % 2]
                        b, wb = acquire(bnk)
                        tmm = None
                        first = True
                        for xs, ws in ((xph, w1h), (xpl, w1h), (xph, w1l)):
                            for c in range(NCH):
                                last3 = (ws is w1l) and (c == NCH - 1)
                                tmm = P.op("tensor",
                                    lambda e, c=c, gc_=gc_, b=b, xs=xs, ws=ws,
                                    first=(first and c == 0), last3=last3:
                                    e.matmul(b[:, :],
                                             ws[:, c, gc_ * 128:(gc_ + 1) * 128],
                                             xs[:, c, :], start=first,
                                             stop=last3),
                                    waits=([t_w1h, t_w1l, tsp] + wb)
                                    if (first and c == 0) else [],
                                    inc="mm" if last3 else None)
                            first = False
                        tg = P.op("scalar", lambda e, b=b, ei=ei, gc_=gc_,
                                  hs=hs: e.activation(
                                      hs, b[:, :], ACT.Gelu,
                                      bias=S["eb1"][:, ei * NHC + gc_:
                                                    ei * NHC + gc_ + 1],
                                      scale=1.0),
                                  waits=[tmm, prev.get(("hs", hc % 2)),
                                         last_eo_prev], inc="aa")
                        release(bnk, tg)
                        tsc = P.op("vector", lambda e, hs=hs, b=b6:
                                   e.tensor_mul(hs, hs, b[:, :]),
                                   waits=[tg, twb], inc="vv")
                        P.wait("vector", [tsc])
                        t_hh = P.op("vector", lambda e, hc=hc, hs=hs:
                                    e.tensor_copy(hhh[:, hc, :], hs),
                                    inc="vv")
                        P.wait("vector", [t_hh])
                        t_hl = P.op("vector", lambda e, hc=hc, hs=hs:
                                    e.tensor_sub(hhl[:, hc, :], hs,
                                                 hhh[:, hc, :]), inc="vv")
                        prev[("hs", hc % 2)] = t_hl
                        h_done = t_hl
                    if half == 1:
                        release(6, h_done)
                    for dc in range(NCH):
                        bnk = (4, 5, 7)[dc % 3]
                        b, wb = acquire(bnk)
                        tmm = None
                        first = True
                        if ei == 0 and half == 0:
                            tmm = P.op("tensor", lambda e, dc=dc, b=b:
                                       e.matmul(b[:, :],
                                                S["eb2"][0:E_EXP,
                                                         dc * 128:(dc + 1) * 128],
                                                S["wfm"][0:E_EXP, :],
                                                start=True, stop=False),
                                       waits=[ld_all, t_wfm] + wb, inc="mm")
                            prev["wfm"] = tmm
                            first = False
                        for xs, ws in ((hhh, w2h), (hhl, w2h), (hhh, w2l)):
                            for c in range(NCH):
                                kk = half * NCH + c
                                last3 = (ws is w2l) and (c == NCH - 1)
                                tmm = P.op("tensor",
                                    lambda e, c=c, kk=kk, dc=dc, b=b, xs=xs,
                                    ws=ws, first=(first and c == 0),
                                    last3=last3: e.matmul(
                                        b[:, :],
                                        ws[:, kk, dc * 128:(dc + 1) * 128],
                                        xs[:, c, :], start=first, stop=last3),
                                    waits=([t_w2h, t_w2l, h_done] + wb)
                                    if (ws is w2h and xs is hhh and c == 0)
                                    else [],
                                    inc="mm" if last3 else None)
                                first = False
                        if ei == 0 and half == 0:
                            tev = P.op("vector", lambda e, b=b, dc=dc:
                                       e.tensor_copy(mac[:, dc, :], b[:, :]),
                                       waits=[tmm, prev.get(("m", dc))],
                                       inc="vv")
                        else:
                            tev = P.op("vector", lambda e, b=b, dc=dc:
                                       e.tensor_add(mac[:, dc, :],
                                                    mac[:, dc, :], b[:, :]),
                                       waits=[tmm, prev.get(("m", dc))],
                                       inc="vv")
                        release(bnk, tev)
                        eo_done = tev
                        prev[("m", dc)] = tev
            t_st = P.dma("gpsimd", macc_sp[:, :, jo * TT:(jo + 1) * TT],
                         AR_X[:, 8192:12288], waits=[eo_done])
            prev["xstg"] = tsp
            prev["xpair"] = tmm
            prev["w1"] = t_w1l
            prev["w2"] = t_w2l
            P.wait("sync", [t_st])
            P.barrier()
        P.barrier()

    if PHASE_LIMIT >= 8:
        sA = AR_W[:, 0:8192].rearrange("p (c n) -> p c n", c=NCH)
        sB = AR_W[:, 8192:16384].rearrange("p (c n) -> p c n", c=NCH)
        sB2 = AR_W[:, 16384:24576].rearrange("p (c n) -> p c n", c=NCH)
        sB3 = AR_W[:, 24576:32768].rearrange("p (c n) -> p c n", c=NCH)
        sY = AR_X[:, 0:8192].rearrange("p (c n) -> p c n", c=NCH)

        def mat_mm(lhs_v, rhs_v, out_v, pre_waits, tag_inplace=False,
                   coeffs=None, iterm=None):
            done = []
            for jh in range(2):
                mm_last = None
                for ic in range(NCH):
                    b, wb = acquire(ic)
                    for kc in range(NCH):
                        mm_last = P.op("tensor",
                            lambda e, ic=ic, kc=kc, jh=jh, b=b: e.matmul(
                                b[:, :], lhs_v[:, kc, ic * 128:(ic + 1) * 128],
                                rhs_v[:, kc, jh * 512:(jh + 1) * 512],
                                start=(kc == 0), stop=(kc == NCH - 1)),
                            waits=(pre_waits + wb) if kc == 0 else [],
                            inc="mm" if kc == NCH - 1 else None)
                if coeffs is None:
                    for ic in range(NCH):
                        tev = P.op("vector", lambda e, ic=ic, jh=jh, b=ps[ic]:
                                   e.tensor_copy(
                                       out_v[:, ic, jh * 512:(jh + 1) * 512],
                                       b[:, :]), waits=[mm_last], inc="vv")
                        release(ic, tev)
                        done.append(tev)
                else:
                    c1, c2, c3 = coeffs
                    p1s = []
                    for ic in range(NCH):
                        tp1 = P.op("vector", lambda e, ic=ic, jh=jh, b=ps[ic],
                                   c1=c1: e.scalar_tensor_tensor(
                                       out_v[:, ic, jh * 512:(jh + 1) * 512],
                                       sB[:, ic, jh * 512:(jh + 1) * 512], c1,
                                       b[:, :], ALU.mult, ALU.add),
                                   waits=[mm_last], inc="vv")
                        release(ic, tp1)
                        p1s.append(tp1)
                    P.wait("vector", [p1s[-1]])
                    p2s = []
                    for ic in range(NCH):
                        tp2 = P.op("vector", lambda e, ic=ic, jh=jh, c2=c2:
                                   e.scalar_tensor_tensor(
                                       out_v[:, ic, jh * 512:(jh + 1) * 512],
                                       sB2[:, ic, jh * 512:(jh + 1) * 512], c2,
                                       out_v[:, ic, jh * 512:(jh + 1) * 512],
                                       ALU.mult, ALU.add), inc="vv")
                        p2s.append(tp2)
                    P.wait("vector", [p2s[-1]])
                    for ic in range(NCH):
                        tp3 = P.op("vector", lambda e, ic=ic, jh=jh, c3=c3:
                                   e.scalar_tensor_tensor(
                                       out_v[:, ic, jh * 512:(jh + 1) * 512],
                                       sB3[:, ic, jh * 512:(jh + 1) * 512], c3,
                                       out_v[:, ic, jh * 512:(jh + 1) * 512],
                                       ALU.mult, ALU.add), inc="vv")
                        done.append(tp3)
                    if iterm is not None:
                        P.wait("vector", [done[-1]])
                        for ic in range(NCH):
                            col = ic * 128
                            if col // 512 != jh:
                                continue
                            tpi = P.op("vector", lambda e, ic=ic, col=col,
                                       c0=iterm: e.scalar_tensor_tensor(
                                           out_v[:, ic, col:col + 128],
                                           S["iden"], c0,
                                           out_v[:, ic, col:col + 128],
                                           ALU.mult, ALU.add), inc="vv")
                            done.append(tpi)
            return done[-1]

        for gi in range(4):
            cs = list(EXPM_A)
            if gi == 3:
                cs = [a * ((-1.0) ** k) for k, a in enumerate(cs)]
            t_a = P.dma("sync", sA, gens[gi])
            t_neg = P.op("vector", lambda e: e.tensor_scalar(
                sB[:, :, :], sA[:, :, :], -1.0, None, ALU.mult),
                waits=[t_a], inc="vv")
            tk2 = mat_mm(sA, sB, sB2, [t_a, t_neg])
            tk3 = mat_mm(sA, sB2, sB3, [tk2])
            tk4 = mat_mm(sA, sB3, sY, [tk3])
            mv = []
            for ic in range(NCH):
                tmv = P.op("vector", lambda e, ic=ic: e.tensor_copy(
                    sA[:, ic, :], sY[:, ic, :]), waits=[tk4], inc="vv")
                mv.append(tmv)
            t_b4 = mv[-1]
            y0 = []
            for ic in range(NCH):
                ty = P.op("vector", lambda e, ic=ic, c13=cs[13]:
                          e.tensor_scalar(sY[:, ic, :], sB[:, ic, :], c13,
                                          None, ALU.mult),
                          waits=[t_b4], inc="vv")
                y0.append(ty)
            P.wait("vector", [y0[-1]])
            y0b = []
            for ic in range(NCH):
                ty = P.op("vector", lambda e, ic=ic, c14=cs[14]:
                          e.scalar_tensor_tensor(sY[:, ic, :], sB2[:, ic, :],
                                                 c14, sY[:, ic, :], ALU.mult,
                                                 ALU.add), inc="vv")
                y0b.append(ty)
            P.wait("vector", [y0b[-1]])
            yi = []
            for ic in range(NCH):
                ty = P.op("vector", lambda e, ic=ic, c12=cs[12]:
                          e.scalar_tensor_tensor(
                              sY[:, ic, ic * 128:(ic + 1) * 128], S["iden"],
                              c12, sY[:, ic, ic * 128:(ic + 1) * 128],
                              ALU.mult, ALU.add), waits=[t_idn], inc="vv")
                yi.append(ty)
            t_y = yi[-1]
            for g in (8, 4, 0):
                t_y = mat_mm(sA, sY, sY, [t_y],
                             coeffs=(cs[g + 1], cs[g + 2], cs[g + 3]),
                             iterm=cs[g])
            P.dma("gpsimd", expm_sp[gi], sY[:, :, :], waits=[t_y])
            P.barrier()
        lhs = AR_W[:, 0:8192].rearrange("p (c n) -> p c n", c=NCH)
        rhs = AR_W[:, 8192:16384].rearrange("p (c n) -> p c n", c=NCH)
        outm = AR_W[:, 16384:24576].rearrange("p (c n) -> p c n", c=NCH)
        t_l = P.dma("sync", lhs, expm_sp[2])
        t_r = P.dma("sync", rhs, expm_sp[3])
        t_p1 = mat_mm(lhs, rhs, outm, [t_l, t_r])
        t_l = P.dma("sync", lhs, expm_sp[1], waits=[t_p1])
        cp = []
        for ic in range(NCH):
            tc2 = P.op("vector", lambda e, ic=ic: e.tensor_copy(
                rhs[:, ic, :], outm[:, ic, :]), waits=[t_p1], inc="vv")
            cp.append(tc2)
        t_p2 = mat_mm(lhs, rhs, outm, [t_l, cp[-1]])
        t_l = P.dma("sync", lhs, expm_sp[0], waits=[t_p2])
        cp = []
        for ic in range(NCH):
            tc2 = P.op("vector", lambda e, ic=ic: e.tensor_copy(
                rhs[:, ic, :], outm[:, ic, :]), waits=[t_p2], inc="vv")
            cp.append(tc2)
        t_rm = mat_mm(lhs, rhs, outm, [t_l, cp[-1]])
        P.barrier()

    if PHASE_LIMIT >= 9:
        rv = AR_W[:, 16384:24576].rearrange("p (c n) -> p c n", c=NCH)
        mt = AR_X[:, 0:4096].rearrange("p (c t) -> p c t", c=NCH)
        nt = AR_X[:, 4096:8192].rearrange("p (c t) -> p c t", c=NCH)
        prev_t = {}
        for jo in range(NT_OWN):
            t_ld = P.dma("sync", mt, macc_sp[:, :, jo * TT:(jo + 1) * TT],
                         waits=[prev_t.get("mm")])
            ev_done = []
            tmm = None
            for dc in range(NCH):
                b, wb = acquire(dc)
                for c in range(NCH):
                    tmm = P.op("tensor", lambda e, c=c, dc=dc, b=b: e.matmul(
                        b[:, :], rv[:, c, dc * 128:(dc + 1) * 128],
                        mt[:, c, :], start=(c == 0), stop=(c == NCH - 1)),
                        waits=([t_ld] + wb) if c == 0 else [],
                        inc="mm" if c == NCH - 1 else None)
                tev = P.op("vector", lambda e, b=b, dc=dc: e.tensor_copy(
                    nt[:, dc, :], b[:, :]),
                    waits=[tmm, prev_t.get("st")], inc="vv")
                release(dc, tev)
                ev_done.append(tev)
            t_st = P.dma("gpsimd", nona_sp[:, :, jo * TT:(jo + 1) * TT],
                         nt[:, :, :], waits=ev_done)
            prev_t = {"mm": tmm, "st": t_st}
        P.barrier()

    if PHASE_LIMIT >= 10:
        ab_v = AR_X[:, 0:4096].rearrange("p (c t) -> p c t", c=NCH)
        no_v = AR_X[:, 4096:8192].rearrange("p (c t) -> p c t", c=NCH)
        tmpd = AR_X[:, 8192:8704]
        xn_v = AR_W[:, 0:4096].rearrange("p (c t) -> p c t", c=NCH)
        at_v = AR_W[:, 4096:8192].rearrange("p (c t) -> p c t", c=NCH)
        x_v = AR_W[:, 8192:12288].rearrange("p (c t) -> p c t", c=NCH)
        ou_v = AR_W[:, 12288:16384].rearrange("p (c t) -> p c t", c=NCH)
        grow = AR_X[0:1, 8704 + TT:8704 + 2 * TT]
        prev_t = {}
        for jo in range(NT_OWN):
            sl = slice(jo * TT, (jo + 1) * TT)
            t_ab = P.dma("sync", ab_v, abel_sp[:, :, sl], waits=[prev_t.get("d")])
            t_no = P.dma("sync", no_v, nona_sp[:, :, sl])
            t_xn = P.dma("sync", xn_v, xn_sp[:, :, TCTX + jo * TT:
                                             TCTX + (jo + 1) * TT])
            t_at = P.dma("sync", at_v, attn_sp[:, :, sl])
            t_x = P.dma("sync", x_v, x_own[:, :, sl])
            b6, w6 = acquire(6)
            tmm = None
            for i in range(16):
                src = ab_v if i < 8 else no_v
                cc = i % 8
                tmm = P.op("tensor", lambda e, i=i, src=src, cc=cc, b=b6:
                           e.matmul(b[0:1, :], S["gcw"][:, i:i + 1],
                                    src[:, cc, :], start=(i == 0),
                                    stop=(i == 15)),
                           waits=([t_ab, t_no, ld_all] + w6) if i == 0 else [],
                           inc="mm" if i == 15 else None)
            tsg = P.op("scalar", lambda e, b=b6: e.activation(
                grow, b[0:1, :], ACT.Sigmoid, bias=S["gcb"][0:1, 0:1],
                scale=1.0), waits=[tmm, prev_t.get("gb")], inc="aa")
            release(6, tsg)
            b7, w7 = acquire(7)
            tgb = P.op("tensor", lambda e, b=b7: e.matmul(
                b[:, :], S["ones_row"], grow, start=True, stop=True),
                waits=[tsg, t_on2] + w7, inc="mm")
            prev_t["gb"] = tgb
            outs = []
            dlast = None
            for c in range(NCH):
                td = P.op("vector", lambda e, c=c: e.tensor_sub(
                    tmpd, no_v[:, c, :], ab_v[:, c, :]),
                    waits=[t_ab, t_no, tgb, dlast], inc="vv")
                P.wait("vector", [td])
                tm2 = P.op("vector", lambda e, c=c, b=b7: e.tensor_mul(
                    tmpd, tmpd, b[:, :]), inc="vv")
                P.wait("vector", [tm2])
                t1 = P.op("vector", lambda e, c=c: e.tensor_add(
                    ou_v[:, c, :], ab_v[:, c, :], tmpd), inc="vv")
                P.wait("vector", [t1])
                t2 = P.op("vector", lambda e, c=c: e.tensor_add(
                    ou_v[:, c, :], ou_v[:, c, :], x_v[:, c, :]),
                    waits=[t_x], inc="vv")
                P.wait("vector", [t2])
                t3 = P.op("vector", lambda e, c=c: e.tensor_add(
                    ou_v[:, c, :], ou_v[:, c, :], xn_v[:, c, :]),
                    waits=[t_xn], inc="vv")
                P.wait("vector", [t3])
                t4 = P.op("vector", lambda e, c=c: e.tensor_add(
                    ou_v[:, c, :], ou_v[:, c, :], at_v[:, c, :]),
                    waits=[t_at], inc="vv")
                dlast = t4
                outs.append(t4)
            release(7, dlast)
            t_st = P.dma("gpsimd", out_t[:, :, sl], ou_v[:, :, :], waits=outs)
            prev_t["d"] = dlast
            prev_t["st"] = t_st
            P.wait("sync", [t_st])
        P.wait("gpsimd", [("gd", P.cnt["gd"])])

    steps = P.steps
    with nc.Block() as block:
        @block.sync
        def _(eng):
            for f in steps["sync"]:
                f(eng)

        @block.tensor
        def _(eng):
            for f in steps["tensor"]:
                f(eng)

        @block.vector
        def _(eng):
            for f in steps["vector"]:
                f(eng)

        @block.scalar
        def _(eng):
            for f in steps["scalar"]:
                f(eng)

        @block.gpsimd
        def _(eng):
            for f in steps["gpsimd"]:
                f(eng)

    stack.close()
    return nc


def host_inputs(inputs):
    x = np.asarray(inputs["L0_attn_x"], dtype=np.float32)

    def pm(w2d, nch):
        K, N = w2d.shape
        return np.ascontiguousarray(
            w2d.reshape(nch, 128, N).transpose(1, 0, 2))

    def colvec(v, nch):
        return np.ascontiguousarray(v.reshape(nch, 128).T)

    sh = {}
    sh["wqkv"] = pm(np.asarray(inputs["L0_attn_Wqkv"], np.float32), NCH)
    sh["bqkv"] = colvec(np.asarray(inputs["L0_attn_bqkv"], np.float32), 24)
    sh["wout"] = pm(np.asarray(inputs["L0_attn_Wout"], np.float32), NCH)
    sh["bout"] = colvec(np.asarray(inputs["L0_attn_bout"], np.float32), NCH)
    sh["wgate"] = pm(np.asarray(inputs["L0_attn_Wgate"], np.float32), NCH)
    sh["bgate"] = colvec(np.asarray(inputs["L0_attn_bgate"], np.float32), NCH)
    sh["pb"] = np.ascontiguousarray(np.broadcast_to(
        np.asarray(inputs["L0_attn_pos_bias"], np.float32).reshape(1, NOFF * H),
        (128, NOFF * H)))
    sh["lng"] = colvec(np.asarray(inputs["L0_ffn_ln_g"], np.float32), NCH)
    sh["lnb"] = colvec(np.asarray(inputs["L0_ffn_ln_b"], np.float32), NCH)
    sh["aw1"] = pm(np.asarray(inputs["L0_ffn_aw1"], np.float32), NCH)
    sh["ab1"] = colvec(np.asarray(inputs["L0_ffn_ab1"], np.float32), 32)
    sh["aw2"] = pm(np.asarray(inputs["L0_ffn_aw2"], np.float32), 32)
    sh["ab2"] = colvec(np.asarray(inputs["L0_ffn_ab2"], np.float32), NCH)
    sh["gatew"] = pm(np.asarray(inputs["L0_ffn_gate_w"], np.float32), NCH)
    sh["gateb"] = np.ascontiguousarray(np.broadcast_to(
        np.asarray(inputs["L0_ffn_gate_b"], np.float32).reshape(1, E_EXP),
        (128, E_EXP)))
    ew1 = np.asarray(inputs["L0_ffn_ew1"], np.float32)
    sh["ew1"] = np.ascontiguousarray(
        ew1.reshape(E_EXP, NCH, 128, HEXP).transpose(0, 2, 1, 3))
    sh["eb1"] = colvec(np.asarray(inputs["L0_ffn_eb1"], np.float32)
                       .reshape(E_EXP * HEXP), E_EXP * NHC)
    ew2 = np.asarray(inputs["L0_ffn_ew2"], np.float32)
    sh["ew2"] = np.ascontiguousarray(
        ew2.reshape(E_EXP, NHC, 128, D).transpose(0, 2, 1, 3))
    sh["eb2"] = np.ascontiguousarray(np.asarray(inputs["L0_ffn_eb2"], np.float32))
    gens = np.asarray(inputs["L0_ffn_gens"], np.float32)
    sh["gens"] = np.ascontiguousarray(
        gens.reshape(4, NCH, 128, D).transpose(0, 2, 1, 3))
    sh["gcw"] = colvec(np.asarray(inputs["L0_ffn_gc_w"], np.float32)
                       .reshape(2 * D), 2 * NCH)
    sh["gcb"] = np.asarray(inputs["L0_ffn_gc_b"], np.float32).reshape(1, 1)

    maps = []
    for core in range(8):
        seq, half = core // 2, core % 2
        xo = pm(np.ascontiguousarray(
            x[seq, half * TQ:(half + 1) * TQ, :].T), NCH)
        xc = np.zeros((128, NCH, TCTX), np.float32)
        mk = np.zeros((128, TCTX), np.float32)
        if half == 1:
            xc[:, :, 1024:] = pm(np.ascontiguousarray(x[seq, 0:TQ, :].T), NCH)
            mk[:, 1024:] = 1.0
        m = {"x_own": xo, "x_ctx": xc, "maskc": mk}
        m.update(sh)
        maps.append(m)
    return maps


def kernel(**inputs):
    nc = build_program()
    maps = host_inputs(inputs)
    res = run_bass_kernel_spmd(nc, maps, core_ids=list(range(8)))
    out = np.zeros((4, 4096, D), dtype=np.float32)
    for core in range(8):
        seq, half = core // 2, core % 2
        o = np.asarray(res.results[core]["out"])
        out[seq, half * TQ:(half + 1) * TQ, :] = \
            o.transpose(1, 0, 2).reshape(D, TQ).T
    return out
